# version 08
# baseline (speedup 1.0000x reference)
import math
import numpy as np

import concourse.bass as bass
import concourse.mybir as mybir
from concourse.bass_utils import run_bass_kernel_spmd
from concourse.bass_core_rust import IndirectOffsetOnAxis

F32 = mybir.dt.float32
BF16 = mybir.dt.bfloat16
U8 = mybir.dt.uint8
AF = mybir.ActivationFunctionType
ALU = mybir.AluOpType
AX = mybir.AxisListType

B = 8
S = 2048
DM = 2048
NH = 16
NKV = 4
HD = 128
GQ = NH // NKV
NE = 32
DI = 768
TOPK = 4
HALF = S // 2
NC_DM = DM // 128
NC_I = DI // 128
SCALE = 1.0 / math.sqrt(HD)
EPS = 1e-5
MASKNEG = -1.0e9
THETA = 10000.0

O_ACC = 0
O_YBF = 65536
O_SCR = 98304
O_XATT = 118784
O_KV = 151552
O_TAB = 184320
O_P = 200704
P_IDB = O_P
P_IDF = O_P + 256
P_TRI = O_P + 768
P_ONES = O_P + 1280
P_WTOK = O_P + 1792
P_WTT = O_P + 2816
P_ST = O_P + 6912
P_RT = O_P + 7936
P_I32 = O_P + 8960
P_ITL = O_P + 9024
ARENA_BYTES = O_P + 9536
CAP = 512


class Sem:
    def __init__(self, h, name):
        self.h = h
        self.name = name
        self.n = 0


class Prog:
    ENGINES = ("tensor", "vector", "scalar", "gpsimd", "sync")

    def __init__(self, nc):
        self.nc = nc
        self.stream = {e: [] for e in self.ENGINES}

    def sem(self, name):
        return Sem(self.nc.alloc_semaphore(name), name)

    def op(self, eng, fn, waits=(), inc=None, k=1):
        for (s, v) in waits:
            assert v <= s.n, f"wait on future value {v}>{s.n} of {s.name}"
        val = None
        if inc is not None:
            inc.n += k
            val = inc.n
        self.stream[eng].append((fn, tuple(waits), inc, k))
        return val

    def t(self, fn, waits=(), inc=None, k=1):
        return self.op("tensor", fn, waits, inc, k)

    def v(self, fn, waits=(), inc=None, k=1):
        return self.op("vector", fn, waits, inc, k)

    def s(self, fn, waits=(), inc=None, k=1):
        return self.op("scalar", fn, waits, inc, k)

    def g(self, fn, waits=(), inc=None, k=1):
        return self.op("gpsimd", fn, waits, inc, k)

    def y(self, fn, waits=(), inc=None, k=1):
        return self.op("sync", fn, waits, inc, k)

    def finalize(self):
        with self.nc.Block() as block:
            for ename in self.ENGINES:
                ops = self.stream[ename]
                if not ops:
                    continue

                def make(ops_):
                    def emit(eng):
                        for fn, waits, inc, k in ops_:
                            for (sm, val) in waits:
                                eng.wait_ge(sm.h, val)
                            instr = fn(eng)
                            if inc is not None:
                                instr.then_inc(inc.h, k)
                    return emit
                getattr(block, ename)(make(ops))


def build_program(debug=None):
    nc = bass.Bass("TRN2", target_bir_lowering=False)

    xT = nc.dram_tensor("xT", [DM, S], F32, kind="ExternalInput").ap()
    wq = nc.dram_tensor("wq", [NH, 128, NC_DM, 128], F32, kind="ExternalInput").ap()
    wk = nc.dram_tensor("wk", [NKV, 128, NC_DM, 128], F32, kind="ExternalInput").ap()
    wv = nc.dram_tensor("wv", [128, NC_DM, NKV * 128], F32, kind="ExternalInput").ap()
    wo = nc.dram_tensor("wo", [NC_DM, 128, NC_DM, 128], F32, kind="ExternalInput").ap()
    rt = nc.dram_tensor("rt", [128, NC_DM, NE], F32, kind="ExternalInput").ap()
    gate = nc.dram_tensor("gate", [NE, NC_I, 128, NC_DM, 128], F32, kind="ExternalInput").ap()
    up = nc.dram_tensor("up", [NE, NC_I, 128, NC_DM, 128], F32, kind="ExternalInput").ap()
    down = nc.dram_tensor("down", [NE, NC_I, 128, DM], F32, kind="ExternalInput").ap()
    cosd = nc.dram_tensor("cosd", [128, S], F32, kind="ExternalInput").ap()
    sind = nc.dram_tensor("sind", [128, S], F32, kind="ExternalInput").ap()
    trid = nc.dram_tensor("trid", [128, 128], F32, kind="ExternalInput").ap()
    identd = nc.dram_tensor("identd", [128, 128], F32, kind="ExternalInput").ap()
    iotad = nc.dram_tensor("iotad", [128, 1], F32, kind="ExternalInput").ap()
    ebased = nc.dram_tensor("ebased", [128, 32], F32, kind="ExternalInput").ap()
    out = nc.dram_tensor("out", [S, DM], F32, kind="ExternalOutput").ap()
    y_dram = nc.dram_tensor("y_dram", [S, DM], BF16, kind="Internal").ap()
    acc_dram = nc.dram_tensor("acc_dram", [S + 128, DM], F32, kind="Internal").ap()
    wtok_dram = nc.dram_tensor("wtok_dram", [S, NE], F32, kind="Internal").ap()
    perm_dram = nc.dram_tensor("perm_dram", [NE * CAP + 128, 1], mybir.dt.int32, kind="Internal").ap()

    arena = nc.alloc_sbuf_tensor("arena", [128, ARENA_BYTES], U8).ap()

    def cv(off, nbytes, dtype):
        return arena[:, off:off + nbytes].bitcast(dtype)

    idb = cv(P_IDB, 256, BF16)
    idf = cv(P_IDF, 512, F32)
    tri = cv(P_TRI, 512, F32)
    ones_r = cv(P_ONES, 512, F32)[0:1, :]
    wtok = cv(P_WTOK, 1024, F32)
    wtokT = cv(P_WTT, 4096, F32)[0:32, :]
    stats = cv(P_ST, 1024, F32)

    def st(i):
        return stats[:, i:i + 1]
    rp = stats[:, 32:64]
    rcur = stats[:, 64:96]
    rkeep = stats[:, 96:128]
    ovm = stats[:, 128:160]
    pos_tm = stats[:, 160:192]
    ebase_sb = stats[:, 192:224]
    iota_sb = stats[:, 14:15]
    cntprev = stats[0:32, 13:14]
    tok_i32 = [cv(P_I32 + 4 * j, 4, mybir.dt.int32) for j in range(2)]
    mj_i32 = [cv(P_I32 + 8 + 4 * j, 4, mybir.dt.int32) for j in range(8)]

    acc = cv(O_ACC, 65536, F32)
    ybf = cv(O_YBF, 32768, BF16)
    kTr = cv(O_KV, 16384, BF16)
    v_tm = cv(O_KV + 16384, 16384, BF16)
    cosf = cv(O_TAB, 8192, F32)
    sinf = cv(O_TAB + 8192, 8192, F32)
    xatt = cv(O_XATT, 32768, BF16)

    PS = nc.alloc_psum_tensor("PS", [128, 4096], F32).ap()

    def bank(i, n=1):
        return PS[:, 512 * i:512 * (i + n)]

    def bank_b16(i):
        return PS[:, 512 * i:512 * (i + 1)].bitcast(BF16)

    P = Prog(nc)
    PR = P.sem("PR")
    TB = P.sem("TB")
    wS = [P.sem("wA"), P.sem("wB"), P.sem("wC")]
    uS = [P.sem("uA"), P.sem("uB"), P.sem("uC")]
    dS = [P.sem("dA"), P.sem("dB")]
    xH = P.sem("xH")
    xR = [P.sem("xRA"), P.sem("xRB")]
    WR = P.sem("WR")
    sD = P.sem("sD")
    OS = [P.sem("OA"), P.sem("OB")]
    DG = P.sem("DG")
    PO = P.sem("PO")
    PE = P.sem("PE")
    AC = P.sem("AC")
    VE = P.sem("VE")

    bank_guard = {}
    bank_ready = {}

    def mm_group(bk, mms, extra_waits=()):
        n = len(mms)
        for i, fn in enumerate(mms):
            waits = []
            if i == 0:
                waits.extend(bank_guard.get(bk, []))
                waits.extend(extra_waits)
            val = P.t(fn, waits=waits, inc=PE if i == n - 1 else None)
        bank_ready[bk] = (PE, val)
        bank_guard[bk] = []
        return val

    def bank_done(bk, sem, val):
        bank_guard.setdefault(bk, []).append((sem, val))

    P.y(lambda e: e.dma_start(out=cosf, in_=cosd), inc=TB, k=16)
    P.y(lambda e: e.dma_start(out=sinf, in_=sind), inc=TB, k=16)
    P.y(lambda e: e.dma_start(out=tri, in_=trid), inc=TB, k=16)
    P.y(lambda e: e.dma_start(out=idf, in_=identd), inc=TB, k=16)
    tabrdy = [[(TB, TB.n)]]
    P.g(lambda e: e.dma_start(out=idb, in_=identd), inc=PR, k=16)
    P.g(lambda e: e.memset(ones_r, 1.0), inc=PR, k=1)
    eps_ve = P.v(lambda e: e.memset(st(24), EPS), inc=VE)

    def xfull_chunk(c):
        if c < 8:
            return xatt[:, 2048 * c:2048 * (c + 1)]
        return ybf[:, 2048 * (c - 8):2048 * (c - 7)]

    xT_r = xT.rearrange("(c p) s -> c p s", p=128)
    for c in range(NC_DM):
        P.g(lambda e, c=c: e.dma_start(out=xfull_chunk(c), in_=xT_r[c]), inc=PR, k=16)

    wv_sb = cv(O_ACC, 16384, BF16)
    wk_sb = [cv(O_ACC + 16384 + 4096 * j, 4096, BF16) for j in range(2)]
    P.g(lambda e: e.dma_start(out=wv_sb, in_=wv.rearrange("p c n -> p (c n)")), inc=PR, k=16)
    rt_sb = cv(P_RT, 1024, BF16)
    P.g(lambda e: e.dma_start(out=rt_sb, in_=rt.rearrange("p c n -> p (c n)")), inc=PR, k=16)
    P.g(lambda e: e.dma_start(out=iota_sb, in_=iotad), inc=PR, k=16)
    P.g(lambda e: e.dma_start(out=ebase_sb, in_=ebased), inc=PR, k=16)
    pr_val = PR.n
    wk_val = [0, 0, 0, 0]

    def load_wk(kf, waits=()):
        P.g(lambda e, kf=kf: e.dma_start(out=wk_sb[kf % 2],
                                         in_=wk[kf].rearrange("p c n -> p (c n)")),
            waits=waits, inc=wS[kf % 2], k=16)
        wk_val[kf] = wS[kf % 2].n

    load_wk(0)
    load_wk(1)

    rot = 0
    for sc in range(16):
        bk = rot % 4
        rot += 1
        mms = []
        for c in range(NC_DM):
            def fn(e, c=c, sc=sc, bk=bk):
                return e.matmul(bank(bk), xfull_chunk(c)[:, 128 * sc:128 * (sc + 1)],
                                wv_sb[:, 512 * c:512 * (c + 1)],
                                start=(c == 0), stop=(c == NC_DM - 1))
            mms.append(fn)
        w = [(PR, pr_val)] if sc == 0 else []
        val = mm_group(bk, mms, extra_waits=w)
        av = P.s(lambda e, sc=sc, bk=bk: e.copy(v_tm[:, 512 * sc:512 * (sc + 1)], bank(bk)),
                 waits=[bank_ready[bk]], inc=AC)
        bank_done(bk, AC, av)
    v_done_AC = AC.n

    swp = cv(O_SCR, 2048, F32)
    t2s = cv(O_SCR + 2048, 2048, F32)
    t3s = cv(O_SCR + 4096, 2048, F32)
    rope_guard = [None]

    def rope_apply(bk, col0, dst):
        waits = [bank_ready[bk]] + ([rope_guard[0]] if rope_guard[0] else [])
        a1 = P.s(lambda e, bk=bk: e.copy(swp[0:64, :], bank(bk)[64:128, :]), waits=waits, inc=AC)
        a2 = P.s(lambda e, bk=bk: e.copy(swp[64:128, :], bank(bk)[0:64, :]), inc=AC)
        colr = slice(col0, col0 + 512)
        v1 = P.v(lambda e, bk=bk, colr=colr: e.tensor_tensor(t2s, bank(bk), cosf[:, colr], op=ALU.mult),
                 waits=[(AC, a2)] + tabrdy[0], inc=VE)
        P.v(lambda e, colr=colr: e.tensor_tensor(t3s, swp, sinf[:, colr], op=ALU.mult),
            waits=[(AC, a2)])
        v3 = P.v(lambda e, dst=dst: e.tensor_tensor(dst, t2s, t3s, op=ALU.add), inc=VE)
        rope_guard[0] = (VE, v3)
        bank_done(bk, VE, v1)
        bank_done(bk, AC, a2)
        return v3

    for kf in range(NKV):
        for scol in range(4):
            bk = rot % 4
            rot += 1
            mms = []
            for c in range(NC_DM):
                def fn(e, c=c, kf=kf, scol=scol, bk=bk):
                    return e.matmul(bank(bk), wk_sb[kf % 2][:, 128 * c:128 * (c + 1)],
                                    xfull_chunk(c)[:, 512 * scol:512 * (scol + 1)],
                                    start=(c == 0), stop=(c == NC_DM - 1))
                mms.append(fn)
            val = mm_group(bk, mms, extra_waits=[(wS[kf % 2], wk_val[kf])])
            if scol == 3 and kf + 2 < NKV:
                load_wk(kf + 2, waits=[(PE, val)])
            dst = kTr[:, 2048 * kf + 512 * scol: 2048 * kf + 512 * (scol + 1)]
            rope_apply(bk, 512 * scol, dst)
    kv_done = (VE, VE.n)

    outr = out.rearrange("(a p) n -> a p n", p=128)

    def dump(ap, blk0, waits):
        per = 2048
        ncols = ap.shape[1]
        nblk = (ncols + per - 1) // per
        for a in range(nblk):
            w = waits if a == 0 else []
            c0, c1 = per * a, min(per * (a + 1), ncols)
            P.g(lambda e, a=a, c0=c0, c1=c1: e.dma_start(
                out=outr[blk0 + a][:, 0:c1 - c0], in_=ap[:, c0:c1]),
                waits=w, inc=DG, k=16)
        return blk0 + nblk

    def finish_debug():
        fin = DG.n
        P.y(lambda e: e.nop(), waits=[(DG, fin)])
        P.finalize()
        return nc

    if debug == "kv":
        b0 = dump(kTr, 0, [kv_done, (AC, v_done_AC)])
        b0 = dump(v_tm, b0, [])
        dump(cosf, b0, tabrdy[0])
        return finish_debug()

    guards = {
        "xatt": [(PE, PE.n)],
        "ybf": [(PE, PE.n)],
        "acc": [(PE, PE.n)],
        "kv": [],
        "tab": [],
        "scr": [kv_done],
    }

    def xh_chunk(c):
        return xatt[:, 1024 * c:1024 * (c + 1)]

    qTr = ybf
    attT = xatt

    wq_sb = [cv(O_SCR + 6144 + 4096 * j, 4096, BF16) for j in range(2)]
    p_sb = [cv(O_SCR, 8192, F32), cv(O_SCR + 8192, 8192, F32)]
    pT_sb = [cv(O_SCR + 4096, 4096, BF16), cv(O_SCR + 8192 + 4096, 4096, BF16)]
    pb_sb = [cv(O_TAB + 4096 * j, 4096, BF16) for j in range(2)]
    xr_sb = [cv(O_SCR + 2048 * j, 2048, F32) for j in range(2)]
    wo_sb = [cv(O_TAB + 4096 * j, 4096, BF16) for j in range(2)]
    tmA = cv(O_SCR + 4096, 8192, F32)
    tmB = cv(O_SCR + 12288, 8192, F32)
    gA = [cv(O_XATT + 12288 * j, 12288, BF16) for j in range(2)]
    uA = [cv(O_XATT + 24576 + 12288 * j, 12288, BF16) for j in range(2)]
    dA = [cv(O_XATT + 49152 + 12288 * j, 12288, BF16) for j in range(2)]
    wrow = cv(O_XATT + 73728, 4096, F32)[0:1, :]
    sg_sb = cv(O_SCR, 2048, F32)
    uw_sb = cv(O_SCR + 2048, 2048, F32)
    w_sb = cv(O_SCR + 4096, 4096, F32)
    hT = cv(O_SCR + 8192, 12288, BF16)

    rot_sc = [0]

    def phase_a2a(h):
        for c in range(NC_DM):
            P.g(lambda e, c=c: e.dma_start(
                out=xh_chunk(c), in_=xT_r[c][:, HALF * h:HALF * (h + 1)]),
                waits=guards["xatt"] if c == 0 else [], inc=xH, k=16)
        xh_val = xH.n
        guards["xatt"] = []
        rope_guard[0] = (VE, VE.n)

        wq_val = [0] * NH

        def load_wq(hc, waits=()):
            P.g(lambda e, hc=hc: e.dma_start(
                out=wq_sb[hc % 2], in_=wq[hc].rearrange("p c n -> p (c n)")),
                waits=waits, inc=wS[hc % 2], k=16)
            wq_val[hc] = wS[hc % 2].n

        load_wq(0, waits=guards["scr"])
        load_wq(1)
        first = True
        for hc in range(NH):
            last_mm = 0
            for scol in range(2):
                bk = rot_sc[0] % 4
                rot_sc[0] += 1
                mms = []
                for c in range(NC_DM):
                    def fn(e, c=c, hc=hc, scol=scol, bk=bk):
                        return e.matmul(bank(bk), wq_sb[hc % 2][:, 128 * c:128 * (c + 1)],
                                        xh_chunk(c)[:, 512 * scol:512 * (scol + 1)],
                                        start=(c == 0), stop=(c == NC_DM - 1))
                    mms.append(fn)
                ew = [(wS[hc % 2], wq_val[hc]), (xH, xh_val)]
                if first:
                    ew += guards["ybf"]
                    first = False
                last_mm = mm_group(bk, mms, extra_waits=ew)
                dst = qTr[:, 1024 * hc + 512 * scol:1024 * hc + 512 * (scol + 1)]
                rope_apply(bk, HALF * h + 512 * scol, dst)
            if hc + 2 < NH:
                load_wq(hc + 2, waits=[(PE, last_mm)])
        guards["ybf"] = []
        return last_mm

    def phase_a2b(h, a2a_last):
        qTr_ready = (VE, VE.n)
        scr_ok = [(PE, a2a_last), (VE, VE.n)] + guards["scr"]
        state = {}
        attval = {}
        flags = {"first_p": True, "first_att": True}

        def stage1(idx, qt, head):
            par = idx % 2
            hkv = head // GQ
            qglob = 8 * h + qt
            L = 128 * (qglob + 1)
            nch = (L + 511) // 512
            maxc = stats[:, 232 + 8 * par:232 + 8 * par + nch]
            sumc = stats[:, 224 + 4 * par:224 + 4 * par + nch]
            rmaxs = st(0 + 4 * par)
            rinvs = st(2 + 4 * par)
            negbs = st(3 + 4 * par)
            banks = []
            mm_vals = []
            for tc in range(nch):
                w = min(512, L - 512 * tc)
                bk = rot_sc[0] % 4
                rot_sc[0] += 1
                banks.append((bk, w))
                waits = list(bank_guard.get(bk, []))
                if idx == 0 and tc == 0:
                    waits.append(qTr_ready)
                    waits.extend(guards["kv"])
                    guards["kv"] = []
                mmv = P.t(lambda e, bk=bk, w=w, head=head, qt=qt, hkv=hkv, tc=tc:
                          e.matmul(bank(bk)[:, 0:w],
                                   qTr[:, 1024 * head + 128 * qt:1024 * head + 128 * (qt + 1)],
                                   kTr[:, 2048 * hkv + 512 * tc:2048 * hkv + 512 * tc + w],
                                   start=True, stop=True),
                          waits=waits, inc=PE)
                bank_ready[bk] = (PE, mmv)
                bank_guard[bk] = []
                mm_vals.append(mmv)
            bkd, wd = banks[-1]
            vv = P.v(lambda e, bkd=bkd, wd=wd: e.tensor_tensor(
                bank(bkd)[:, wd - 128:wd], bank(bkd)[:, wd - 128:wd], tri, op=ALU.add),
                waits=[(PE, mm_vals[-1])] + tabrdy[0], inc=VE)
            for tc in range(nch):
                bk, w = banks[tc]
                vv = P.v(lambda e, bk=bk, w=w, tc=tc, maxc=maxc: e.reduce_max(
                    maxc[:, tc:tc + 1], bank(bk)[:, 0:w], axis=AX.X),
                    waits=[(PE, mm_vals[tc])] if tc < nch - 1 else [(VE, vv)], inc=VE)
            vv = P.v(lambda e, maxc=maxc, rmaxs=rmaxs: e.reduce_max(rmaxs, maxc, axis=AX.X),
                     waits=[(VE, vv)], inc=VE)
            vneg = P.v(lambda e, rmaxs=rmaxs, negbs=negbs: e.tensor_scalar(
                negbs, rmaxs, -SCALE, None, op0=ALU.mult), waits=[(VE, vv)], inc=VE)
            flags_scr = []
            if flags["first_p"]:
                flags_scr = scr_ok
                flags["first_p"] = False
            aexp = 0
            for tc in range(nch):
                bk, w = banks[tc]
                ew = [(VE, vneg)] + (flags_scr if tc == 0 else [])
                if tc == 0 and idx - 2 in attval:
                    ew.append((PE, attval[idx - 2]))
                aexp = P.s(lambda e, par=par, tc=tc, w=w, bk=bk, negbs=negbs, sumc=sumc:
                           e.activation(p_sb[par][:, 512 * tc:512 * tc + w],
                                        bank(bk)[:, 0:w], AF.Exp,
                                        bias=negbs, scale=SCALE,
                                        accum_out=sumc[:, tc:tc + 1]),
                           waits=ew, inc=AC)
                bank_done(bk, AC, aexp)
            aexp = P.s(lambda e: e.copy(st(25), st(25)), waits=[(AC, aexp)], inc=AC)
            vv = P.v(lambda e, sumc=sumc: e.reduce_sum(st(1 + 4 * par), sumc, axis=AX.X),
                     waits=[(AC, aexp)], inc=VE)
            vrc = P.v(lambda e, rinvs=rinvs: e.reciprocal(rinvs, st(1 + 4 * par)),
                      waits=[(VE, vv)], inc=VE)
            vw = [(VE, vrc)]
            if idx - 2 in attval:
                vw.append((PE, attval[idx - 2]))
            vpb = P.v(lambda e, par=par, L=L, rinvs=rinvs: e.tensor_scalar(
                pb_sb[par][:, 0:L], p_sb[par][:, 0:L], rinvs, None, op0=ALU.mult),
                waits=vw, inc=VE)
            state[idx] = dict(par=par, hkv=hkv, qt=qt, head=head, L=L, vpb=vpb)

        def stage2(idx):
            stt_ = state.pop(idx)
            par, hkv, qt, head, L, vpb = (stt_["par"], stt_["hkv"], stt_["qt"],
                                          stt_["head"], stt_["L"], stt_["vpb"])
            ntb = L // 128
            ev_ac = ev_ve = 0
            for a in range((ntb + 3) // 4):
                tbk = 4 + (a % 2)
                n_in_batch = min(4, ntb - 4 * a)
                for j in range(n_in_batch):
                    tb = 4 * a + j
                    waits = []
                    if j == 0:
                        waits = list(bank_guard.get(("pt", tbk), []))
                        if a == 0:
                            waits.append((VE, vpb))
                            if idx == 0:
                                waits.append((PR, pr_val))
                    tv = P.t(lambda e, par=par, tb=tb, tbk=tbk, j=j:
                             e.transpose(bank_b16(tbk)[:, 128 * j:128 * (j + 1)],
                                         pb_sb[par][:, 128 * tb:128 * (tb + 1)], idb),
                             waits=waits, inc=PE if j == n_in_batch - 1 else None)
                if a % 2 == 0:
                    ev = P.s(lambda e, par=par, a=a, tbk=tbk, n=n_in_batch:
                             e.copy(pT_sb[par][:, 512 * a:512 * a + 128 * n],
                                    bank_b16(tbk)[:, 0:128 * n]),
                             waits=[(PE, tv)], inc=AC)
                    bank_guard[("pt", tbk)] = [(AC, ev)]
                    ev_ac = ev
                else:
                    ev = P.v(lambda e, par=par, a=a, tbk=tbk, n=n_in_batch:
                             e.tensor_copy(pT_sb[par][:, 512 * a:512 * a + 128 * n],
                                           bank_b16(tbk)[:, 0:128 * n]),
                             waits=[(PE, tv)], inc=VE)
                    bank_guard[("pt", tbk)] = [(VE, ev)]
                    ev_ve = ev
            pt_done = ([(AC, ev_ac)] if ev_ac else []) + ([(VE, ev_ve)] if ev_ve else [])
            abk = 6 + (idx % 2)
            mms = []
            for tb in range(ntb):
                def fn(e, tb=tb, par=par, hkv=hkv, abk=abk, ntb=ntb):
                    return e.matmul(bank(abk)[:, 0:128],
                                    v_tm[:, 512 * tb + 128 * hkv:512 * tb + 128 * (hkv + 1)],
                                    pT_sb[par][:, 128 * tb:128 * (tb + 1)],
                                    start=(tb == 0), stop=(tb == ntb - 1))
                mms.append(fn)
            attv = mm_group(abk, mms, extra_waits=pt_done)
            attval[idx] = attv
            ew = [bank_ready[abk]]
            if flags["first_att"]:
                ew.append((PE, a2a_last))
                flags["first_att"] = False
            aev = P.s(lambda e, head=head, qt=qt, abk=abk:
                      e.copy(attT[:, 1024 * head + 128 * qt:1024 * head + 128 * (qt + 1)],
                             bank(abk)[:, 0:128]),
                      waits=ew, inc=AC)
            bank_done(abk, AC, aev)
            return attv

        items = [(qt, head) for qt in range(8) for head in range(NH)]
        pend = None
        last_att_pe = 0
        for idx, (qt, head) in enumerate(items):
            stage1(idx, qt, head)
            if pend is not None:
                last_att_pe = stage2(pend)
            pend = idx
        last_att_pe = stage2(pend)
        return last_att_pe, AC.n

    def phase_a3(h, attT_AC, a2b_last):
        scrw = [(PE, a2b_last), (VE, VE.n)] + guards["scr"]
        guards["scr"] = []
        wo_val = [0] * NC_DM

        def load_wo(fo, waits=()):
            P.g(lambda e, fo=fo: e.dma_start(
                out=wo_sb[fo % 2], in_=wo[fo].rearrange("p c n -> p (c n)")),
                waits=waits, inc=wS[fo % 2], k=16)
            wo_val[fo] = wS[fo % 2].n

        load_wo(0, waits=guards["tab"])
        load_wo(1)
        guards["tab"] = []
        xr_rot = [0]
        xr_guard = [None, None]
        xr_v = [0, 0]
        first = True
        last_mm = 0
        last_add = 0
        for fo in range(NC_DM):
            for scol in range(2):
                jr = xr_rot[0] % 2
                xr_rot[0] += 1
                xw = [xr_guard[jr]] if xr_guard[jr] else []
                P.y(lambda e, fo=fo, scol=scol, jr=jr, h=h: e.dma_start(
                    out=xr_sb[jr],
                    in_=xT_r[fo][:, HALF * h + 512 * scol:HALF * h + 512 * (scol + 1)]),
                    waits=xw + (scrw if first else []), inc=xR[jr], k=16)
                xr_v[jr] = xR[jr].n
                xr_val = (xR[jr], xr_v[jr])
                bk = rot_sc[0] % 4
                rot_sc[0] += 1
                mms = []
                for c in range(NC_DM):
                    def fn(e, c=c, fo=fo, scol=scol, bk=bk):
                        return e.matmul(bank(bk), wo_sb[fo % 2][:, 128 * c:128 * (c + 1)],
                                        attT[:, 1024 * c + 512 * scol:1024 * c + 512 * (scol + 1)],
                                        start=(c == 0), stop=(c == NC_DM - 1))
                    mms.append(fn)
                ew = [(wS[fo % 2], wo_val[fo])] + ([(AC, attT_AC)] if scol == 0 else [])
                if first:
                    first = False
                last_mm = mm_group(bk, mms, extra_waits=ew)
                aw = [(PE, last_mm), xr_val]
                if fo == 0 and scol == 0:
                    aw += guards["acc"]
                    guards["acc"] = []
                last_add = P.v(lambda e, fo=fo, scol=scol, bk=bk, jr=jr: e.tensor_tensor(
                    acc[:, 1024 * fo + 512 * scol:1024 * fo + 512 * (scol + 1)],
                    bank(bk), xr_sb[jr], op=ALU.add),
                    waits=aw, inc=VE)
                bank_done(bk, VE, last_add)
                xr_guard[jr] = (VE, last_add)
            if fo + 2 < NC_DM:
                load_wo(fo + 2, waits=[(PE, last_mm)])
        if debug == "a3pre":
            dump(acc, 0, [(VE, last_add)])
            return None
        ydram_v = [0, 0]
        ln_last_tb = [0]
        prev_tb_pe = [a2b_last]
        ybf_first = [True]
        last_ybf_ve = [0]

        def layer_norm(h, write_back, a3_last_add):
            out_dma_val = [0, 0]
            for sch in range(8):
                tm = tmA if sch % 2 == 0 else tmB
                other = tmB if sch % 2 == 0 else tmA
                tin = 4 if sch % 2 == 0 else 6
                tout = 5 if sch % 2 == 0 else 7
                evv = 0
                for a in range(4):
                    for j in range(4):
                        f = 4 * a + j
                        waits = []
                        if j == 0:
                            waits = list(bank_guard.get(tin, []))
                            if a == 0 and sch == 0:
                                waits.append((VE, a3_last_add))
                            if a == 0 and sch >= 2:
                                waits.append((PE, ln_last_tb[0]))
                                waits.append((xR[sch % 2], ydram_v[sch % 2]))
                        tv = P.t(lambda e, f=f, sch=sch, tin=tin, j=j:
                                 e.transpose(bank(tin)[:, 128 * j:128 * (j + 1)],
                                             acc[:, 1024 * f + 128 * sch:1024 * f + 128 * (sch + 1)],
                                             idf),
                                 waits=waits, inc=PE if j == 3 else None)
                    evw = [(PE, tv)]
                    if sch == 0 and a == 0:
                        evw += scrw
                    evv = P.s(lambda e, a=a, tm=tm, tin=tin:
                              e.copy(tm[:, 512 * a:512 * (a + 1)], bank(tin)),
                              waits=evw, inc=AC)
                    bank_guard[tin] = [(AC, evv)]
                vs1 = P.v(lambda e, tm=tm: e.reduce_sum(st(16), tm, axis=AX.X),
                          waits=[(AC, evv)], inc=VE)
                sqw = [(PE, prev_tb_pe[0])] + (scrw if sch == 0 else [])
                if sch >= 1:
                    sqw.append((xR[(sch + 1) % 2], ydram_v[(sch + 1) % 2]))
                asq = P.s(lambda e, tm=tm, other=other: e.activation(
                    other, tm, AF.Square, accum_out=st(17)), waits=sqw, inc=AC)
                asq = P.s(lambda e: e.copy(st(25), st(25)), waits=[(AC, asq)], inc=AC)
                v1 = P.v(lambda e: e.tensor_scalar(st(18), st(16), 1.0 / DM, None, op0=ALU.mult),
                         waits=[(VE, vs1)], inc=VE)
                v2 = P.v(lambda e: e.tensor_scalar(st(19), st(17), 1.0 / DM, None, op0=ALU.mult),
                         waits=[(AC, asq), (VE, v1)], inc=VE)
                v3 = P.v(lambda e: e.tensor_tensor(st(20), st(18), st(18), op=ALU.mult),
                         waits=[(VE, v2)], inc=VE)
                vvar = P.v(lambda e: e.tensor_tensor(st(21), st(19), st(20), op=ALU.subtract),
                           waits=[(VE, v3)], inc=VE)
                P.s(lambda e: e.activation(st(22), st(21), AF.Ln, bias=st(24)),
                    waits=[(VE, vvar)])
                arstd = P.s(lambda e: e.activation(st(23), st(22), AF.Exp, scale=-0.5),
                            inc=AC)
                vnorm = P.v(lambda e, tm=tm: e.tensor_scalar(
                    tm, tm, st(18), st(23), op0=ALU.subtract, op1=ALU.mult),
                    waits=[(AC, arstd)], inc=VE)
                row0 = HALF * h + 128 * sch
                P.g(lambda e, row0=row0, tm=tm: e.dma_start(
                    out=y_dram[row0:row0 + 128, :], in_=tm),
                    waits=[(VE, vnorm)], inc=xR[sch % 2], k=16)
                P.y(lambda e, row0=row0, tm=tm: e.dma_start(
                    out=acc_dram[row0:row0 + 128, :], in_=tm),
                    waits=[(VE, vnorm)], inc=xR[sch % 2], k=16)
                ydram_v[sch % 2] = xR[sch % 2].n
                if debug == "ln1" and h == 0:
                    P.v(lambda e, sch=sch: e.tensor_copy(
                        wtok[:, 16 * sch:16 * sch + 9], stats[:, 16:25]),
                        waits=[(VE, vnorm)], inc=VE)
                if write_back:
                    for a in range(4):
                        for j in range(4):
                            f = 4 * a + j
                            waits = []
                            if j == 0:
                                waits = list(bank_guard.get(tout, []))
                                if a == 0:
                                    waits.append((VE, vnorm))
                            tv = P.t(lambda e, f=f, tm=tm, tout=tout, j=j:
                                     e.transpose(bank(tout)[:, 128 * j:128 * (j + 1)],
                                                 tm[:, 128 * f:128 * (f + 1)], idf),
                                     waits=waits, inc=PE if j == 3 else None)
                        ln_last_tb[0] = tv
                        accv = acc.rearrange("p (f c) -> p f c", f=NC_DM)
                        ybfv = ybf.rearrange("p (f c) -> p f c", f=NC_DM)
                        psv = bank(tout).rearrange("p (f c) -> p f c", f=4)
                        af = P.s(lambda e, a=a, sch=sch, accv=accv, psv=psv:
                                 e.copy(accv[:, 4 * a:4 * (a + 1), 128 * sch:128 * (sch + 1)], psv),
                                 waits=[(PE, tv)], inc=AC)
                        yw = [(AC, af)]
                        if ybf_first[0]:
                            yw += [(PE, a2b_last)]
                            ybf_first[0] = False
                        yv = P.v(lambda e, a=a, sch=sch, ybfv=ybfv, psv=psv:
                                 e.tensor_copy(ybfv[:, 4 * a:4 * (a + 1), 128 * sch:128 * (sch + 1)], psv),
                                 waits=yw, inc=VE)
                        last_ybf_ve[0] = yv
                        bank_guard[tout] = [(VE, yv)]
                    prev_tb_pe[0] = ln_last_tb[0]
            return last_ybf_ve[0]

        layer_norm(h, True, last_add)
        return last_mm, last_ybf_ve[0], ln_last_tb[0], layer_norm

    def phase_router(h, ybf_ready_ve):
        wtok_ve = []
        first = True
        for sch in range(8):
            mms = []
            for c in range(NC_DM):
                def fn(e, c=c, sch=sch):
                    return e.matmul(bank(0)[:, 0:32],
                                    ybf[:, 1024 * c + 128 * sch:1024 * c + 128 * (sch + 1)],
                                    rt_sb[:, 32 * c:32 * (c + 1)],
                                    start=(c == 0), stop=(c == NC_DM - 1))
                mms.append(fn)
            ew = [(VE, ybf_ready_ve), (PR, pr_val)] if first else []
            first = False
            mmv = mm_group(0, mms, extra_waits=ew)
            vm = P.v(lambda e: e.reduce_max(st(8), bank(0)[:, 0:32], axis=AX.X),
                     waits=[(PE, mmv)], inc=VE)
            vn = P.v(lambda e: e.tensor_scalar(st(9), st(8), -1.0, None, op0=ALU.mult),
                     waits=[(VE, vm)], inc=VE)
            ae = P.s(lambda e: e.activation(rp, bank(0)[:, 0:32], AF.Exp, bias=st(9)),
                     waits=[(VE, vn)], inc=AC)
            bank_done(0, AC, ae)
            bank_done(0, VE, vn)
            vq = P.v(lambda e: e.tensor_copy(rcur, rp), waits=[(AC, ae)], inc=VE)
            for r in range(4):
                vq = P.v(lambda e: e.reduce_max(st(10), rcur, axis=AX.X),
                         waits=[(VE, vq)], inc=VE)
                if r < 3:
                    vq = P.v(lambda e: e.tensor_scalar(rkeep, rcur, st(10), None, op0=ALU.is_ge),
                             waits=[(VE, vq)], inc=VE)
                    vq = P.v(lambda e: e.scalar_tensor_tensor(rcur, rkeep, -2.0, rcur,
                                                              op0=ALU.mult, op1=ALU.add),
                             waits=[(VE, vq)], inc=VE)
            vq = P.v(lambda e: e.tensor_scalar(rkeep, rp, st(10), None, op0=ALU.is_ge),
                     waits=[(VE, vq)], inc=VE)
            vq = P.v(lambda e: e.tensor_tensor(rp, rp, rkeep, op=ALU.mult),
                     waits=[(VE, vq)], inc=VE)
            vq = P.v(lambda e: e.reduce_sum(st(11), rp, axis=AX.X), waits=[(VE, vq)], inc=VE)
            vq = P.v(lambda e: e.reciprocal(st(12), st(11)), waits=[(VE, vq)], inc=VE)
            wv_ = P.v(lambda e, sch=sch: e.tensor_scalar(
                wtok[:, 32 * sch:32 * (sch + 1)], rp, st(12), None, op0=ALU.mult),
                waits=[(VE, vq)], inc=VE)
            wtok_ve.append(wv_)
            row0 = HALF * h + 128 * sch
            P.g(lambda e, sch=sch, row0=row0: e.dma_start(
                out=wtok_dram[row0:row0 + 128, :], in_=wtok[:, 32 * sch:32 * (sch + 1)]),
                waits=[(VE, wv_)], inc=sD, k=16)
        wtt_ac = 0
        for sch in range(8):
            waits = list(bank_guard.get(1, []))
            if sch == 0:
                waits.append((VE, wtok_ve[-1]))
            tv = P.t(lambda e, sch=sch: e.transpose(
                bank(1)[0:32, 0:128], wtok[:, 32 * sch:32 * (sch + 1)], idf),
                waits=waits, inc=PE)
            bank_ready[1] = (PE, tv)
            wtt_ac = P.s(lambda e, sch=sch: e.copy(
                wtokT[:, 128 * sch:128 * (sch + 1)], bank(1)[0:32, 0:128]),
                waits=[(PE, tv)], inc=AC)
            bank_guard[1] = [(AC, wtt_ac)]
        return wtt_ac

    sD_init = [0]

    def perm_init():
        itile = cv(P_ITL, 512, mybir.dt.int32)
        vi = P.v(lambda e: e.memset(itile, 10 ** 9), inc=VE)
        P.g(lambda e: e.dma_start(
            out=perm_dram[0:NE * CAP].rearrange("(p a) one -> p (a one)", p=128), in_=itile),
            waits=[(VE, vi)], inc=sD, k=16)
        sD_init[0] = sD.n

    def phase_idxbuild(h, wtt_ac):
        vk = P.v(lambda e: e.tensor_scalar(wtokT, wtokT, 0.0, None, op0=ALU.is_gt),
                 waits=[(AC, wtt_ac)], inc=VE)
        init = 0.0 if h == 0 else cntprev
        vscan = P.v(lambda e, init=init: e.tensor_tensor_scan(
            wtokT, wtokT, wtokT, init, op0=ALU.add, op1=ALU.bypass),
            waits=[(VE, vk)], inc=VE)
        vcnt = P.v(lambda e: e.tensor_copy(cntprev, wtokT[:, 1023:1024]),
                   waits=[(VE, vscan)], inc=VE)
        P.v(lambda e: e.tensor_scalar(st(15), iota_sb, float(NE * CAP), None, op0=ALU.add),
            waits=[(VE, vcnt)])
        scat_prev = [0, 0]
        pos_read = [0]
        for sc in range(8):
            scg = 8 * h + sc
            tbk = 6 + sc % 2
            tw = list(bank_guard.get(tbk, []))
            if sc == 0:
                tw.append((VE, vcnt))
            tv = P.t(lambda e, sc=sc, tbk=tbk: e.transpose(
                bank(tbk)[:, 0:32], wtokT[:, 128 * sc:128 * (sc + 1)], idf[0:32, 0:32]),
                waits=tw, inc=PE)
            bank_ready[tbk] = (PE, tv)
            pv = P.s(lambda e, tbk=tbk: e.copy(pos_tm, bank(tbk)[:, 0:32]),
                     waits=[(PE, tv), (VE, pos_read[0])], inc=AC)
            bank_guard[tbk] = [(AC, pv)]
            vv = P.v(lambda e, sc=sc: e.tensor_scalar(
                rcur, wtok[:, 32 * sc:32 * (sc + 1)], 0.0, None, op0=ALU.is_gt),
                waits=[(AC, pv)], inc=VE)
            vv = P.v(lambda e: e.tensor_scalar(ovm, pos_tm, CAP + 0.5, None, op0=ALU.is_le),
                     waits=[(VE, vv)], inc=VE)
            vv = P.v(lambda e: e.tensor_tensor(rcur, rcur, ovm, op=ALU.mult),
                     waits=[(VE, vv)], inc=VE)
            vv = P.v(lambda e: e.tensor_tensor(rp, pos_tm, ebase_sb, op=ALU.add),
                     waits=[(VE, vv)], inc=VE)
            pos_read[0] = vv
            vv = P.v(lambda e: e.tensor_scalar(rp, rp, -1.0, None, op0=ALU.add),
                     waits=[(VE, vv)], inc=VE)
            vv = P.v(lambda e: e.tensor_scalar(rkeep, rcur, -1.0e9, 1.0e9,
                                               op0=ALU.mult, op1=ALU.add),
                     waits=[(VE, vv)], inc=VE)
            vv = P.v(lambda e: e.tensor_tensor(rp, rp, rcur, op=ALU.mult),
                     waits=[(VE, vv)], inc=VE)
            vv = P.v(lambda e: e.tensor_tensor(rp, rp, rkeep, op=ALU.add),
                     waits=[(VE, vv)], inc=VE)
            vv = P.v(lambda e, scg=scg: e.tensor_scalar(
                st(9), iota_sb, float(128 * scg), None, op0=ALU.add),
                waits=[(VE, vv)], inc=VE)
            tkw = [(VE, vv)]
            if sc >= 2:
                tkw.append((sD, scat_prev[sc % 2]))
            vv = P.v(lambda e, sc=sc: e.tensor_copy(tok_i32[sc % 2], st(9)),
                     waits=tkw, inc=VE)
            for j in range(4):
                sl = 4 * (sc % 2) + j
                vv = P.v(lambda e: e.tensor_reduce(st(8), rp, axis=AX.X, op=ALU.min),
                         waits=[(VE, vv)], inc=VE)
                vv = P.v(lambda e: e.tensor_tensor(st(12), st(8), st(15), op=ALU.min),
                         waits=[(VE, vv)], inc=VE)
                vv = P.v(lambda e, sl=sl: e.tensor_copy(mj_i32[sl], st(12)),
                         waits=[(VE, vv)], inc=VE)
                vv = P.v(lambda e: e.tensor_scalar(rkeep, rp, st(8), None, op0=ALU.is_le),
                         waits=[(VE, vv)], inc=VE)
                vv = P.v(lambda e: e.scalar_tensor_tensor(rp, rkeep, 1.0e9, rp,
                                                          op0=ALU.mult, op1=ALU.add),
                         waits=[(VE, vv)], inc=VE)
                P.g(lambda e, sc=sc, sl=sl: e.indirect_dma_start(
                    perm_dram, IndirectOffsetOnAxis(mj_i32[sl], 0), tok_i32[sc % 2], None),
                    waits=[(VE, vv), (sD, sD_init[0])], inc=sD, k=16)
            scat_prev[sc % 2] = sD.n
        return sD.n

    def phase_sparse_b(bpool, ydram_rdy, wtok_rdy, perm_rdy, accdram_rdy, nexp=NE):
        NHF = NC_I // 2
        gA = [cv(12288 * sl_, 12288, BF16) for sl_ in range(3)]
        uA = [cv(36864 + 12288 * sl_, 12288, BF16) for sl_ in range(3)]
        dA = [cv(73728 + 12288 * ih, 12288, BF16) for ih in range(2)]
        y_gath = [cv(98304 + 16384 * par, 16384, BF16) for par in range(2)]
        yT_gath = cv(131072, 16384, BF16)
        hT2 = cv(147456, 6144, BF16)
        out_sb = [cv(153600 + 8192 * tp, 8192, F32) for tp in range(4)]
        wg_sb = [cv(186368 + 512 * par, 512, F32) for par in range(2)]
        idxi = [cv(187392 + 32 * par, 16, mybir.dt.int32) for par in range(2)]
        idxg = [cv(187456 + 32 * par, 16, mybir.dt.int32) for par in range(2)]
        idxa = [cv(187520 + 32 * par, 16, mybir.dt.int32) for par in range(2)]
        idxf = cv(187584, 16, F32)
        sg2 = cv(187648, 2048, F32)
        P.v(lambda e: e.tensor_scalar(st(11), iota_sb, float(S), None, op0=ALU.add))

        g_v = {}
        u_v = {}
        d_v = {}
        gu_guard = [[], [], []]
        dn_guard = [[], []]
        gath_guard = [[], []]
        idx_v = [0, 0]
        gath_v = [0, 0]
        scat_last = [0]

        def load_gu(e_, ih, waits=()):
            sl = slice(NHF * ih, NHF * (ih + 1))
            sb = (2 * e_ + ih) % 3
            P.g(lambda en, e_=e_, sl=sl, sb=sb: en.dma_start(
                out=gA[sb].rearrange("p (i cn) -> p i cn", i=NHF),
                in_=gate[e_, sl].rearrange("i p c n -> p i (c n)")),
                waits=list(waits) + gu_guard[sb], inc=wS[sb], k=16)
            g_v[(e_, ih)] = (wS[sb], wS[sb].n)
            P.g(lambda en, e_=e_, sl=sl, sb=sb: en.dma_start(
                out=uA[sb].rearrange("p (i cn) -> p i cn", i=NHF),
                in_=up[e_, sl].rearrange("i p c n -> p i (c n)")),
                inc=uS[sb], k=16)
            u_v[(e_, ih)] = (uS[sb], uS[sb].n)

        def load_dn(e_, ih, waits=()):
            sl = slice(NHF * ih, NHF * (ih + 1))
            P.g(lambda en, e_=e_, sl=sl, ih=ih: en.dma_start(
                out=dA[ih].rearrange("p (i f) -> p i f", i=NHF),
                in_=down[e_, sl].rearrange("i p f -> p i f")),
                waits=list(waits) + dn_guard[ih], inc=dS[ih], k=16)
            d_v[(e_, ih)] = dS[ih].n

        def load_gathers(e_, waits=()):
            par = e_ % 2
            for t in range(4):
                P.y(lambda en, e_=e_, t=t: en.dma_start(
                    out=idxi[par][:, t:t + 1],
                    in_=perm_dram[CAP * e_ + 128 * t:CAP * e_ + 128 * (t + 1)]),
                    waits=(list(waits) + gath_guard[par]) if t == 0 else [],
                    inc=xR[par], k=16)
            idx_v[par] = xR[par].n
            vv = P.v(lambda en, par=par: en.tensor_copy(idxf, idxi[par]),
                     waits=[(xR[par], idx_v[par])], inc=VE)
            vv = P.v(lambda en, par=par: en.tensor_scalar(idxa[par].bitcast(F32), idxf,
                                                          st(11), None, op0=ALU.min),
                     waits=[(VE, vv)], inc=VE)
            vv = P.v(lambda en, par=par: en.tensor_copy(idxa[par], idxa[par].bitcast(F32)),
                     waits=[(VE, vv)], inc=VE)
            vv = P.v(lambda en: en.tensor_scalar(idxf, idxf, float(S - 1), None,
                                                 op0=ALU.min), waits=[(VE, vv)], inc=VE)
            vv = P.v(lambda en, par=par: en.tensor_copy(idxg[par], idxf),
                     waits=[(VE, vv)], inc=VE)
            for t in range(4):
                P.g(lambda en, par=par, t=t: en.indirect_dma_start(
                    y_gath[par][:, 2048 * t:2048 * (t + 1)], None,
                    y_dram, IndirectOffsetOnAxis(idxg[par][:, t:t + 1], 0)),
                    waits=([(VE, vv)] + list(ydram_rdy) if t == 0 else []),
                    inc=OS[par], k=16)
            for t in range(4):
                P.g(lambda en, par=par, t=t: en.indirect_dma_start(
                    wg_sb[par][:, 32 * t:32 * (t + 1)], None,
                    wtok_dram, IndirectOffsetOnAxis(idxg[par][:, t:t + 1], 0)),
                    waits=(list(wtok_rdy) if t == 0 else []), inc=OS[par], k=16)
            gath_v[par] = OS[par].n

        first_w = [(sD, perm_rdy)]
        load_gathers(0, waits=first_w)
        load_gathers(1)
        load_gu(0, 0, waits=bpool)
        load_gu(0, 1)
        load_gu(1, 0)
        load_dn(0, 0)
        load_dn(0, 1)
        last_acc = 0
        for e_ in range(nexp):
            par = e_ % 2
            tpos = 0
            for t in range(4):
                for a in range(4):
                    tbk = (t * 4 + a) % 4
                    for j in range(4):
                        c = 4 * a + j
                        tw = []
                        if j == 0:
                            tw = list(bank_guard.get(tbk, []))
                            if t == 0 and a == 0:
                                tw += [(OS[par], gath_v[par])] + gu_guard[par]
                        tv = P.t(lambda en, par=par, t=t, c=c, tbk=tbk, j=j:
                                 en.transpose(bank_b16(tbk)[:, 128 * j:128 * (j + 1)],
                                              y_gath[par][:, 2048 * t + 128 * c:
                                                          2048 * t + 128 * (c + 1)], idb),
                                 waits=tw, inc=PE if j == 3 else None)
                    ytv = yT_gath.rearrange("p (c s) -> p c s", c=NC_DM)
                    psv = bank_b16(tbk)[:, 0:512].rearrange("p (c s) -> p c s", c=4)
                    ev = P.s(lambda en, par=par, t=t, a=a, ytv=ytv, psv=psv:
                             en.copy(ytv[:, 4 * a:4 * (a + 1), 128 * t:128 * (t + 1)], psv),
                             waits=[(PE, tv)], inc=AC)
                    bank_guard[tbk] = [(AC, ev)]
            tr_last = tv
            yT_rdy = AC.n
            last_h_ve = 0
            for ih in range(2):
                last_gu = 0
                for il in range(NHF):
                    i = NHF * ih + il
                    gbk = (i % 2) * 2
                    ubk = gbk + 1
                    mg = []
                    mu = []
                    sb = (2 * e_ + ih) % 3
                    for c in range(NC_DM):
                        def fg(en, c=c, il=il, sb=sb, par=par, gbk=gbk):
                            return en.matmul(bank(gbk),
                                             gA[sb][:, 2048 * il + 128 * c:2048 * il + 128 * (c + 1)],
                                             yT_gath[:, 512 * c:512 * (c + 1)],
                                             start=(c == 0), stop=(c == NC_DM - 1))
                        def fu(en, c=c, il=il, sb=sb, par=par, ubk=ubk):
                            return en.matmul(bank(ubk),
                                             uA[sb][:, 2048 * il + 128 * c:2048 * il + 128 * (c + 1)],
                                             yT_gath[:, 512 * c:512 * (c + 1)],
                                             start=(c == 0), stop=(c == NC_DM - 1))
                        mg.append(fg)
                        mu.append(fu)
                    ew = [g_v[(e_, ih)]]
                    if il == 0 and ih == 0:
                        ew.append((AC, yT_rdy))
                    gv = mm_group(gbk, mg, extra_waits=ew)
                    uv = mm_group(ubk, mu, extra_waits=[u_v[(e_, ih)]])
                    last_gu = uv
                    asil = P.s(lambda en, gbk=gbk: en.activation(sg2, bank(gbk), AF.Silu),
                               waits=[(PE, gv), (VE, last_h_ve)], inc=AC)
                    bank_done(gbk, AC, asil)
                    last_h_ve = P.v(lambda en, i=i, ubk=ubk: en.tensor_tensor(
                        hT2[:, 512 * i:512 * (i + 1)], sg2, bank(ubk), op=ALU.mult),
                        waits=[(PE, uv), (AC, asil)], inc=VE)
                    bank_done(ubk, VE, last_h_ve)
                gu_guard[(2 * e_ + ih) % 3] = [(PE, last_gu)]
                nh = 2 * e_ + ih + 3
                if nh // 2 < nexp:
                    load_gu(nh // 2, nh % 2, waits=[(PE, last_gu)])
            last_dn = 0
            wr_prev = scat_last[0]
            for t in range(4):
                ev_vals = []
                for fc in range(4):
                    bk = 4 + fc
                    mms = []
                    for ic in range(NC_I):
                        def fn(en, ic=ic, t=t, fc=fc, par=par, bk=bk):
                            return en.matmul(bank(bk),
                                             hT2[:, 512 * ic + 128 * t:512 * ic + 128 * (t + 1)],
                                             dA[ic // NHF][:, 2048 * (ic % NHF) + 512 * fc:
                                                           2048 * (ic % NHF) + 512 * (fc + 1)],
                                             start=(ic == 0), stop=(ic == NC_I - 1))
                        mms.append(fn)
                    ew = [(dS[0], d_v[(e_, 0)]), (dS[1], d_v[(e_, 1)])]
                    if t == 0 and fc == 0:
                        ew.append((VE, last_h_ve))
                    dv = mm_group(bk, mms, extra_waits=ew)
                    last_dn = dv
                    vw = [(PE, dv)] + ([(WR, wr_prev)] if fc == 0 else [])
                    vv = P.s(lambda en, t=t, fc=fc, par=par, bk=bk, e_=e_:
                             en.activation(out_sb[t][:, 512 * fc:512 * (fc + 1)],
                                           bank(bk), AF.Copy,
                                           scale=wg_sb[par][:, 32 * t + e_:32 * t + e_ + 1]),
                             waits=vw, inc=AC)
                    bank_done(bk, AC, vv)
                    ev_vals.append(vv)
                sw = [(AC, ev_vals[-1]), (WR, wr_prev)]
                if e_ == 0 and t == 0:
                    sw += list(accdram_rdy)
                P.g(lambda en, t=t, par=par: en.indirect_dma_start(
                    acc_dram, IndirectOffsetOnAxis(idxa[par][:, t:t + 1], 0),
                    out_sb[t], None, compute_op=ALU.add),
                    waits=sw, inc=WR, k=16)
            scat_last[0] = WR.n
            dn_guard[0] = [(PE, last_dn)]
            dn_guard[1] = [(PE, last_dn)]
            gath_guard[par] = [(PE, tr_last), (WR, scat_last[0])]
            if e_ + 2 < nexp:
                load_gathers(e_ + 2)
            if e_ + 1 < nexp:
                load_dn(e_ + 1, 0)
                load_dn(e_ + 1, 1)
        return scat_last[0], PE.n

    def phase_ln2_dram(scat_done, b_pe):
        tm2 = [cv(16384 * p_, 8192, F32) for p_ in range(2)]
        out_v = [0, 0]
        in_v = [0, 0]
        for sch in range(16):
            par = sch % 2
            tm = tm2[par]
            other = tm2[1 - par]
            iw = [(WR, scat_done), (PE, b_pe)] if sch < 2 else                  [(OS[par], out_v[par])]
            P.y(lambda e, sch=sch, tm=tm: e.dma_start(
                out=tm, in_=acc_dram[128 * sch:128 * (sch + 1), :]),
                waits=iw, inc=xR[par], k=16)
            in_v[par] = xR[par].n
            vs1 = P.v(lambda e, tm=tm: e.reduce_sum(st(16), tm, axis=AX.X),
                      waits=[(xR[par], in_v[par])], inc=VE)
            sqw = [(OS[1 - par], out_v[1 - par])] if sch >= 1 else [(PE, b_pe)]
            sqw += [(xR[par], in_v[par])]
            asq = P.s(lambda e, tm=tm, other=other: e.activation(
                other, tm, AF.Square, accum_out=st(17)), waits=sqw, inc=AC)
            asq = P.s(lambda e: e.copy(st(25), st(25)), waits=[(AC, asq)], inc=AC)
            v1 = P.v(lambda e: e.tensor_scalar(st(18), st(16), 1.0 / DM, None, op0=ALU.mult),
                     waits=[(VE, vs1)], inc=VE)
            v2 = P.v(lambda e: e.tensor_scalar(st(19), st(17), 1.0 / DM, None, op0=ALU.mult),
                     waits=[(AC, asq), (VE, v1)], inc=VE)
            v3 = P.v(lambda e: e.tensor_tensor(st(20), st(18), st(18), op=ALU.mult),
                     waits=[(VE, v2)], inc=VE)
            vvar = P.v(lambda e: e.tensor_tensor(st(21), st(19), st(20), op=ALU.subtract),
                       waits=[(VE, v3)], inc=VE)
            P.s(lambda e: e.activation(st(22), st(21), AF.Ln, bias=st(24)), waits=[(VE, vvar)])
            arstd = P.s(lambda e: e.activation(st(23), st(22), AF.Exp, scale=-0.5), inc=AC)
            vnorm = P.v(lambda e, tm=tm: e.tensor_scalar(
                tm, tm, st(18), st(23), op0=ALU.subtract, op1=ALU.mult),
                waits=[(AC, arstd)], inc=VE)
            P.y(lambda e, sch=sch, tm=tm: e.dma_start(out=outr[sch], in_=tm),
                waits=[(VE, vnorm)], inc=OS[par], k=16)
            out_v[par] = OS[par].n

    perm_init()
    a2a_last = phase_a2a(0)
    if debug == "qTr":
        b0 = dump(qTr, 0, [(VE, VE.n)])
        return finish_debug()
    if debug == "att00":
        phase_a2b(0, a2a_last)
        return finish_debug()
    a2b_last, attT_AC = phase_a2b(0, a2a_last)
    guards["tab"] = [(PE, a2b_last), (VE, VE.n)]
    if debug == "attT":
        b0 = dump(attT, 0, [(AC, attT_AC)])
        return finish_debug()
    if debug == "a3pre":
        phase_a3(0, attT_AC, a2b_last)
        return finish_debug()
    a3_last_mm, ybf_ve, ln_tb, layer_norm_fn = phase_a3(0, attT_AC, a2b_last)
    if debug == "ln1":
        b0 = dump(acc, 0, [(PE, ln_tb), (AC, AC.n), (VE, VE.n)])
        dump(ybf, b0, [(VE, ybf_ve)])
        return finish_debug()
    wtt_ac = phase_router(0, ybf_ve)
    if debug == "wtok":
        b0 = dump(wtok, 0, [(VE, VE.n)])
        P.g(lambda e: e.dma_start(out=outr[b0][0:32, 0:1024], in_=wtokT),
            waits=[(AC, wtt_ac)], inc=DG, k=16)
        return finish_debug()
    phase_idxbuild(0, wtt_ac)

    guards["xatt"] = [(PE, PE.n)]
    guards["ybf"] = [(PE, PE.n)]
    guards["acc"] = [(PE, PE.n)]
    guards["scr"] = [(PE, PE.n), (VE, VE.n)]
    P.y(lambda e: e.dma_start(out=cosf, in_=cosd), waits=[(PE, a2b_last)], inc=TB, k=16)
    P.y(lambda e: e.dma_start(out=sinf, in_=sind), inc=TB, k=16)
    guards["tab"] = guards["tab"] + [(TB, TB.n)]
    tabrdy[0] = [(TB, TB.n)]
    guards["kv"] = []

    a2a_last = phase_a2a(1)
    a2b_last, attT_AC = phase_a2b(1, a2a_last)
    guards["tab"] = [(PE, a2b_last), (VE, VE.n)]
    a3_last_mm, ybf_ve, ln_tb, _ = phase_a3(1, attT_AC, a2b_last)
    wtt_ac = phase_router(1, ybf_ve)
    perm_rdy = phase_idxbuild(1, wtt_ac)
    if debug == "perm":
        P.g(lambda e: e.dma_start(
            out=out[0:8, :], in_=perm_dram[0:NE * CAP].rearrange("(r c) one -> r (c one)", r=8)),
            waits=[(sD, perm_rdy)], inc=DG, k=16)
        P.g(lambda e: e.dma_start(out=out[8:40, 0:1024], in_=wtokT),
            waits=[(VE, VE.n)], inc=DG, k=16)
        return finish_debug()

    bpool = [(PE, PE.n), (AC, AC.n), (VE, VE.n),
             (xR[0], xR[0].n), (xR[1], xR[1].n)]
    ydram_rdy = [(xR[0], xR[0].n), (xR[1], xR[1].n)]
    wtok_rdy = [(sD, sD.n)]
    accdram_rdy = [(xR[0], xR[0].n), (xR[1], xR[1].n)]
    nexp = 1 if debug == "acc1" else NE
    scat_done, b_pe = phase_sparse_b(bpool, ydram_rdy, wtok_rdy, perm_rdy,
                                     accdram_rdy, nexp=nexp)
    if debug == "acc1":
        for a_ in range(16):
            P.g(lambda e, a_=a_: e.dma_start(
                out=outr[a_], in_=acc_dram[128 * a_:128 * (a_ + 1), :]),
                waits=[(WR, scat_done)] if a_ == 0 else [], inc=DG, k=16)
        return finish_debug()
    phase_ln2_dram(scat_done, b_pe)

    P.y(lambda e: e.nop(), waits=[(OS[0], OS[0].n), (OS[1], OS[1].n)])
    P.finalize()
    return nc



def _host_prep(L0_attn_x, L0_attn_Wqkv, L0_attn_Wo, L0_ffn_router_w,
               L0_ffn_gate_w, L0_ffn_up_w, L0_ffn_down_w):
    f32 = np.float32
    Wqkv = np.asarray(L0_attn_Wqkv, f32)
    Wo = np.asarray(L0_attn_Wo, f32)
    router = np.asarray(L0_ffn_router_w, f32)
    gate_w = np.asarray(L0_ffn_gate_w, f32)
    up_w = np.asarray(L0_ffn_up_w, f32)
    down_w = np.asarray(L0_ffn_down_w, f32)

    com = {}
    wq_m = Wqkv[:, :NH * HD]
    com["wq"] = np.ascontiguousarray(
        wq_m.reshape(NC_DM, 128, NH, 128).transpose(2, 1, 0, 3))
    wk_m = Wqkv[:, NH * HD:(NH + NKV) * HD]
    com["wk"] = np.ascontiguousarray(
        wk_m.reshape(NC_DM, 128, NKV, 128).transpose(2, 1, 0, 3))
    wv_m = Wqkv[:, (NH + NKV) * HD:]
    com["wv"] = np.ascontiguousarray(
        wv_m.reshape(NC_DM, 128, NKV * 128).transpose(1, 0, 2))
    com["wo"] = np.ascontiguousarray(
        Wo.reshape(NC_DM, 128, NC_DM, 128).transpose(2, 1, 0, 3))
    com["rt"] = np.ascontiguousarray(
        router.T.reshape(NC_DM, 128, NE).transpose(1, 0, 2))
    com["gate"] = np.ascontiguousarray(
        gate_w.reshape(NE, NC_I, 128, NC_DM, 128).transpose(0, 1, 4, 3, 2))
    com["up"] = np.ascontiguousarray(
        up_w.reshape(NE, NC_I, 128, NC_DM, 128).transpose(0, 1, 4, 3, 2))
    com["down"] = np.ascontiguousarray(
        down_w.reshape(NE, DM, NC_I, 128).transpose(0, 2, 3, 1))

    j = np.arange(64, dtype=np.float64)
    freqs = 1.0 / THETA ** (2.0 * j / HD)
    ang = np.arange(S, dtype=np.float64)[None, :] * freqs[:, None]
    cos64 = np.cos(ang).astype(f32)
    sin64 = np.sin(ang).astype(f32)
    com["cosd"] = np.concatenate([cos64, cos64], axis=0)
    com["sind"] = np.concatenate([-sin64, sin64], axis=0)
    qi = np.arange(128)[:, None]
    tj = np.arange(128)[None, :]
    com["trid"] = np.where(tj > qi, np.float32(MASKNEG), np.float32(0.0)).astype(f32)
    com["identd"] = np.eye(128, dtype=f32)
    com["iotad"] = np.arange(128, dtype=f32)[:, None]
    com["ebased"] = np.broadcast_to(
        (np.arange(NE, dtype=f32) * CAP)[None, :], (128, NE)).copy()

    x = np.asarray(L0_attn_x, f32)
    xTs = [np.ascontiguousarray(x[b].T) for b in range(B)]
    return com, xTs


def kernel(L0_attn_x, L0_attn_Wqkv, L0_attn_Wo, L0_ffn_router_w,
           L0_ffn_gate_w, L0_ffn_up_w, L0_ffn_down_w):
    com, xTs = _host_prep(L0_attn_x, L0_attn_Wqkv, L0_attn_Wo, L0_ffn_router_w,
                          L0_ffn_gate_w, L0_ffn_up_w, L0_ffn_down_w)
    nc = build_program()
    in_maps = [dict(com, xT=xTs[b]) for b in range(B)]
    res = run_bass_kernel_spmd(nc, in_maps, core_ids=list(range(B)))
    return np.stack([r["out"] for r in res.results], axis=0)


# version 09
# speedup vs baseline: 1.0246x; 1.0246x over previous
import math
import numpy as np

import concourse.bass as bass
import concourse.mybir as mybir
from concourse.bass_utils import run_bass_kernel_spmd
from concourse.bass_core_rust import IndirectOffsetOnAxis

F32 = mybir.dt.float32
BF16 = mybir.dt.bfloat16
U8 = mybir.dt.uint8
AF = mybir.ActivationFunctionType
ALU = mybir.AluOpType
AX = mybir.AxisListType

B = 8
S = 2048
DM = 2048
NH = 16
NKV = 4
HD = 128
GQ = NH // NKV
NE = 32
DI = 768
TOPK = 4
HALF = S // 2
NC_DM = DM // 128
NC_I = DI // 128
SCALE = 1.0 / math.sqrt(HD)
EPS = 1e-5
MASKNEG = -1.0e9
THETA = 10000.0

O_ACC = 0
O_YBF = 65536
O_SCR = 98304
O_XATT = 118784
O_KV = 151552
O_TAB = 184320
O_P = 200704
P_IDB = O_P
P_IDF = O_P + 256
P_TRI = O_P + 768
P_ONES = O_P + 1280
P_WTOK = O_P + 1792
P_WTT = O_P + 2816
P_ST = O_P + 6912
P_RT = O_P + 7936
P_I32 = O_P + 8960
P_ITL = O_P + 9024
ARENA_BYTES = O_P + 9536
CAP = 512


class Sem:
    def __init__(self, h, name):
        self.h = h
        self.name = name
        self.n = 0


class Prog:
    ENGINES = ("tensor", "vector", "scalar", "gpsimd", "sync")

    def __init__(self, nc):
        self.nc = nc
        self.stream = {e: [] for e in self.ENGINES}

    def sem(self, name):
        return Sem(self.nc.alloc_semaphore(name), name)

    def op(self, eng, fn, waits=(), inc=None, k=1):
        for (s, v) in waits:
            assert v <= s.n, f"wait on future value {v}>{s.n} of {s.name}"
        val = None
        if inc is not None:
            inc.n += k
            val = inc.n
        self.stream[eng].append((fn, tuple(waits), inc, k))
        return val

    def t(self, fn, waits=(), inc=None, k=1):
        return self.op("tensor", fn, waits, inc, k)

    def v(self, fn, waits=(), inc=None, k=1):
        return self.op("vector", fn, waits, inc, k)

    def s(self, fn, waits=(), inc=None, k=1):
        return self.op("scalar", fn, waits, inc, k)

    def g(self, fn, waits=(), inc=None, k=1):
        return self.op("gpsimd", fn, waits, inc, k)

    def y(self, fn, waits=(), inc=None, k=1):
        return self.op("sync", fn, waits, inc, k)

    def finalize(self):
        with self.nc.Block() as block:
            for ename in self.ENGINES:
                ops = self.stream[ename]
                if not ops:
                    continue

                def make(ops_):
                    def emit(eng):
                        for fn, waits, inc, k in ops_:
                            for (sm, val) in waits:
                                eng.wait_ge(sm.h, val)
                            instr = fn(eng)
                            if inc is not None:
                                instr.then_inc(inc.h, k)
                    return emit
                getattr(block, ename)(make(ops))


def build_program(debug=None):
    nc = bass.Bass("TRN2", target_bir_lowering=False)

    xT = nc.dram_tensor("xT", [DM, S], F32, kind="ExternalInput").ap()
    wq = nc.dram_tensor("wq", [NH, 128, NC_DM, 128], F32, kind="ExternalInput").ap()
    wk = nc.dram_tensor("wk", [NKV, 128, NC_DM, 128], F32, kind="ExternalInput").ap()
    wv = nc.dram_tensor("wv", [128, NC_DM, NKV * 128], F32, kind="ExternalInput").ap()
    wo = nc.dram_tensor("wo", [NC_DM, 128, NC_DM, 128], F32, kind="ExternalInput").ap()
    rt = nc.dram_tensor("rt", [128, NC_DM, NE], F32, kind="ExternalInput").ap()
    gate = nc.dram_tensor("gate", [NE, NC_I, 128, NC_DM, 128], F32, kind="ExternalInput").ap()
    up = nc.dram_tensor("up", [NE, NC_I, 128, NC_DM, 128], F32, kind="ExternalInput").ap()
    down = nc.dram_tensor("down", [NE, NC_I, 128, DM], F32, kind="ExternalInput").ap()
    cosd = nc.dram_tensor("cosd", [128, S], F32, kind="ExternalInput").ap()
    sind = nc.dram_tensor("sind", [128, S], F32, kind="ExternalInput").ap()
    trid = nc.dram_tensor("trid", [128, 128], F32, kind="ExternalInput").ap()
    identd = nc.dram_tensor("identd", [128, 128], F32, kind="ExternalInput").ap()
    iotad = nc.dram_tensor("iotad", [128, 1], F32, kind="ExternalInput").ap()
    ebased = nc.dram_tensor("ebased", [128, 32], F32, kind="ExternalInput").ap()
    out = nc.dram_tensor("out", [S, DM], F32, kind="ExternalOutput").ap()
    y_dram = nc.dram_tensor("y_dram", [S, DM], BF16, kind="Internal").ap()
    acc_dram = nc.dram_tensor("acc_dram", [S + 128, DM], F32, kind="Internal").ap()
    wtok_dram = nc.dram_tensor("wtok_dram", [S, NE], F32, kind="Internal").ap()
    perm_dram = nc.dram_tensor("perm_dram", [NE * CAP + 128, 1], mybir.dt.int32, kind="Internal").ap()

    arena = nc.alloc_sbuf_tensor("arena", [128, ARENA_BYTES], U8).ap()

    def cv(off, nbytes, dtype):
        return arena[:, off:off + nbytes].bitcast(dtype)

    idb = cv(P_IDB, 256, BF16)
    idf = cv(P_IDF, 512, F32)
    tri = cv(P_TRI, 512, F32)
    ones_r = cv(P_ONES, 512, F32)[0:1, :]
    wtok = cv(P_WTOK, 1024, F32)
    wtokT = cv(P_WTT, 4096, F32)[0:32, :]
    stats = cv(P_ST, 1024, F32)

    def st(i):
        return stats[:, i:i + 1]
    rp = stats[:, 32:64]
    rcur = stats[:, 64:96]
    rkeep = stats[:, 96:128]
    ovm = stats[:, 128:160]
    pos_tm = stats[:, 160:192]
    ebase_sb = stats[:, 192:224]
    iota_sb = stats[:, 14:15]
    cntprev = stats[0:32, 13:14]
    tok_i32 = [cv(P_I32 + 4 * j, 4, mybir.dt.int32) for j in range(2)]
    mj_i32 = [cv(P_I32 + 8 + 4 * j, 4, mybir.dt.int32) for j in range(8)]

    acc = cv(O_ACC, 65536, F32)
    ybf = cv(O_YBF, 32768, BF16)
    kTr = cv(O_KV, 16384, BF16)
    v_tm = cv(O_KV + 16384, 16384, BF16)
    cosf = cv(O_TAB, 8192, F32)
    sinf = cv(O_TAB + 8192, 8192, F32)
    xatt = cv(O_XATT, 32768, BF16)

    PS = nc.alloc_psum_tensor("PS", [128, 4096], F32).ap()

    def bank(i, n=1):
        return PS[:, 512 * i:512 * (i + n)]

    def bank_b16(i):
        return PS[:, 512 * i:512 * (i + 1)].bitcast(BF16)

    P = Prog(nc)
    PR = P.sem("PR")
    TB = P.sem("TB")
    wS = [P.sem("wA"), P.sem("wB"), P.sem("wC")]
    uS = [P.sem("uA"), P.sem("uB"), P.sem("uC")]
    dS = [P.sem("dA"), P.sem("dB")]
    xH = P.sem("xH")
    xR = [P.sem("xRA"), P.sem("xRB")]
    WR = P.sem("WR")
    sD = P.sem("sD")
    OS = [P.sem("OA"), P.sem("OB")]
    DG = P.sem("DG")
    PO = P.sem("PO")
    PE = P.sem("PE")
    AC = P.sem("AC")
    VE = P.sem("VE")

    bank_guard = {}
    bank_ready = {}

    def mm_group(bk, mms, extra_waits=()):
        n = len(mms)
        for i, fn in enumerate(mms):
            waits = []
            if i == 0:
                waits.extend(bank_guard.get(bk, []))
                waits.extend(extra_waits)
            val = P.t(fn, waits=waits, inc=PE if i == n - 1 else None)
        bank_ready[bk] = (PE, val)
        bank_guard[bk] = []
        return val

    def bank_done(bk, sem, val):
        bank_guard.setdefault(bk, []).append((sem, val))

    P.y(lambda e: e.dma_start(out=cosf, in_=cosd), inc=TB, k=16)
    P.y(lambda e: e.dma_start(out=sinf, in_=sind), inc=TB, k=16)
    P.y(lambda e: e.dma_start(out=tri, in_=trid), inc=TB, k=16)
    P.y(lambda e: e.dma_start(out=idf, in_=identd), inc=TB, k=16)
    tabrdy = [[(TB, TB.n)]]
    P.g(lambda e: e.dma_start(out=idb, in_=identd), inc=PR, k=16)
    eps_ve = P.v(lambda e: e.memset(st(24), EPS), inc=VE)

    def xfull_chunk(c):
        if c < 8:
            return xatt[:, 2048 * c:2048 * (c + 1)]
        return ybf[:, 2048 * (c - 8):2048 * (c - 7)]

    xT_r = xT.rearrange("(c p) s -> c p s", p=128)
    for c in range(NC_DM):
        P.g(lambda e, c=c: e.dma_start(out=xfull_chunk(c), in_=xT_r[c]), inc=PR, k=16)

    wv_sb = cv(O_ACC, 16384, BF16)
    wk_sb = [cv(O_ACC + 16384 + 4096 * j, 4096, BF16) for j in range(2)]
    P.g(lambda e: e.dma_start(out=wv_sb, in_=wv.rearrange("p c n -> p (c n)")), inc=PR, k=16)
    rt_sb = cv(P_RT, 1024, BF16)
    P.g(lambda e: e.dma_start(out=rt_sb, in_=rt.rearrange("p c n -> p (c n)")), inc=PR, k=16)
    P.g(lambda e: e.dma_start(out=iota_sb, in_=iotad), inc=PR, k=16)
    P.g(lambda e: e.dma_start(out=ebase_sb, in_=ebased), inc=PR, k=16)
    pr_val = PR.n
    wk_val = [0, 0, 0, 0]

    def load_wk(kf, waits=()):
        P.g(lambda e, kf=kf: e.dma_start(out=wk_sb[kf % 2],
                                         in_=wk[kf].rearrange("p c n -> p (c n)")),
            waits=waits, inc=wS[kf % 2], k=16)
        wk_val[kf] = wS[kf % 2].n

    load_wk(0)
    load_wk(1)

    rot = 0
    for sc in range(16):
        bk = rot % 4
        rot += 1
        mms = []
        for c in range(NC_DM):
            def fn(e, c=c, sc=sc, bk=bk):
                return e.matmul(bank(bk), xfull_chunk(c)[:, 128 * sc:128 * (sc + 1)],
                                wv_sb[:, 512 * c:512 * (c + 1)],
                                start=(c == 0), stop=(c == NC_DM - 1))
            mms.append(fn)
        w = [(PR, pr_val)] if sc == 0 else []
        val = mm_group(bk, mms, extra_waits=w)
        av = P.s(lambda e, sc=sc, bk=bk: e.copy(v_tm[:, 512 * sc:512 * (sc + 1)], bank(bk)),
                 waits=[bank_ready[bk]], inc=AC)
        bank_done(bk, AC, av)
    v_done_AC = AC.n

    swp = cv(O_SCR, 2048, F32)
    t2s = cv(O_SCR + 2048, 2048, F32)
    t3s = cv(O_SCR + 4096, 2048, F32)
    rope_guard = [None]

    def rope_apply(bk, col0, dst):
        waits = [bank_ready[bk]] + ([rope_guard[0]] if rope_guard[0] else [])
        a1 = P.s(lambda e, bk=bk: e.copy(swp[0:64, :], bank(bk)[64:128, :]), waits=waits, inc=AC)
        a2 = P.s(lambda e, bk=bk: e.copy(swp[64:128, :], bank(bk)[0:64, :]), inc=AC)
        colr = slice(col0, col0 + 512)
        v1 = P.v(lambda e, bk=bk, colr=colr: e.tensor_tensor(t2s, bank(bk), cosf[:, colr], op=ALU.mult),
                 waits=[(AC, a2)] + tabrdy[0], inc=VE)
        P.v(lambda e, colr=colr: e.tensor_tensor(t3s, swp, sinf[:, colr], op=ALU.mult),
            waits=[(AC, a2)])
        v3 = P.v(lambda e, dst=dst: e.tensor_tensor(dst, t2s, t3s, op=ALU.add), inc=VE)
        rope_guard[0] = (VE, v3)
        bank_done(bk, VE, v1)
        bank_done(bk, AC, a2)
        return v3

    for kf in range(NKV):
        for scol in range(4):
            bk = rot % 4
            rot += 1
            mms = []
            for c in range(NC_DM):
                def fn(e, c=c, kf=kf, scol=scol, bk=bk):
                    return e.matmul(bank(bk), wk_sb[kf % 2][:, 128 * c:128 * (c + 1)],
                                    xfull_chunk(c)[:, 512 * scol:512 * (scol + 1)],
                                    start=(c == 0), stop=(c == NC_DM - 1))
                mms.append(fn)
            val = mm_group(bk, mms, extra_waits=[(wS[kf % 2], wk_val[kf])])
            if scol == 3 and kf + 2 < NKV:
                load_wk(kf + 2, waits=[(PE, val)])
            dst = kTr[:, 2048 * kf + 512 * scol: 2048 * kf + 512 * (scol + 1)]
            rope_apply(bk, 512 * scol, dst)
    kv_done = (VE, VE.n)

    outr = out.rearrange("(a p) n -> a p n", p=128)

    def dump(ap, blk0, waits):
        per = 2048
        ncols = ap.shape[1]
        nblk = (ncols + per - 1) // per
        for a in range(nblk):
            w = waits if a == 0 else []
            c0, c1 = per * a, min(per * (a + 1), ncols)
            P.g(lambda e, a=a, c0=c0, c1=c1: e.dma_start(
                out=outr[blk0 + a][:, 0:c1 - c0], in_=ap[:, c0:c1]),
                waits=w, inc=DG, k=16)
        return blk0 + nblk

    def finish_debug():
        fin = DG.n
        P.y(lambda e: e.nop(), waits=[(DG, fin)])
        P.finalize()
        return nc

    if debug == "kv":
        b0 = dump(kTr, 0, [kv_done, (AC, v_done_AC)])
        b0 = dump(v_tm, b0, [])
        dump(cosf, b0, tabrdy[0])
        return finish_debug()

    guards = {
        "xatt": [(PE, PE.n)],
        "ybf": [(PE, PE.n)],
        "acc": [(PE, PE.n)],
        "kv": [],
        "tab": [],
        "scr": [kv_done],
    }

    def xh_chunk(c):
        return xatt[:, 1024 * c:1024 * (c + 1)]

    qTr = ybf
    attT = xatt

    wq_sb = [cv(O_SCR + 6144 + 4096 * j, 4096, BF16) for j in range(2)]
    p_sb = [cv(O_SCR, 8192, F32), cv(O_SCR + 8192, 8192, F32)]
    pT_sb = [cv(O_SCR + 4096, 4096, BF16), cv(O_SCR + 8192 + 4096, 4096, BF16)]
    pb_sb = [cv(O_TAB + 4096 * j, 4096, BF16) for j in range(2)]
    xr_sb = [cv(O_SCR + 2048 * j, 2048, F32) for j in range(2)]
    wo_sb = [cv(O_TAB + 4096 * j, 4096, BF16) for j in range(2)]
    tmA = cv(O_SCR + 4096, 8192, F32)
    tmB = cv(O_SCR + 12288, 8192, F32)
    gA = [cv(O_XATT + 12288 * j, 12288, BF16) for j in range(2)]
    uA = [cv(O_XATT + 24576 + 12288 * j, 12288, BF16) for j in range(2)]
    dA = [cv(O_XATT + 49152 + 12288 * j, 12288, BF16) for j in range(2)]
    wrow = cv(O_XATT + 73728, 4096, F32)[0:1, :]
    sg_sb = cv(O_SCR, 2048, F32)
    uw_sb = cv(O_SCR + 2048, 2048, F32)
    w_sb = cv(O_SCR + 4096, 4096, F32)
    hT = cv(O_SCR + 8192, 12288, BF16)

    rot_sc = [0]

    def phase_a2a(h):
        for c in range(NC_DM):
            P.g(lambda e, c=c: e.dma_start(
                out=xh_chunk(c), in_=xT_r[c][:, HALF * h:HALF * (h + 1)]),
                waits=guards["xatt"] if c == 0 else [], inc=xH, k=16)
        xh_val = xH.n
        guards["xatt"] = []
        rope_guard[0] = (VE, VE.n)

        wq_val = [0] * NH

        def load_wq(hc, waits=()):
            P.g(lambda e, hc=hc: e.dma_start(
                out=wq_sb[hc % 2], in_=wq[hc].rearrange("p c n -> p (c n)")),
                waits=waits, inc=wS[hc % 2], k=16)
            wq_val[hc] = wS[hc % 2].n

        load_wq(0, waits=guards["scr"])
        load_wq(1)
        first = True
        for hc in range(NH):
            last_mm = 0
            for scol in range(2):
                bk = rot_sc[0] % 4
                rot_sc[0] += 1
                mms = []
                for c in range(NC_DM):
                    def fn(e, c=c, hc=hc, scol=scol, bk=bk):
                        return e.matmul(bank(bk), wq_sb[hc % 2][:, 128 * c:128 * (c + 1)],
                                        xh_chunk(c)[:, 512 * scol:512 * (scol + 1)],
                                        start=(c == 0), stop=(c == NC_DM - 1))
                    mms.append(fn)
                ew = [(wS[hc % 2], wq_val[hc]), (xH, xh_val)]
                if first:
                    ew += guards["ybf"]
                    first = False
                last_mm = mm_group(bk, mms, extra_waits=ew)
                dst = qTr[:, 1024 * hc + 512 * scol:1024 * hc + 512 * (scol + 1)]
                rope_apply(bk, HALF * h + 512 * scol, dst)
            if hc + 2 < NH:
                load_wq(hc + 2, waits=[(PE, last_mm)])
        guards["ybf"] = []
        return last_mm

    def phase_a2b(h, a2a_last):
        qTr_ready = (VE, VE.n)
        scr_ok = [(PE, a2a_last), (VE, VE.n)] + guards["scr"]
        state = {}
        attval = {}
        flags = {"first_p": True, "first_att": True}

        def stage1(idx, qt, head):
            par = idx % 2
            hkv = head // GQ
            qglob = 8 * h + qt
            L = 128 * (qglob + 1)
            nch = (L + 511) // 512
            maxc = stats[:, 232 + 8 * par:232 + 8 * par + nch]
            sumc = stats[:, 224 + 4 * par:224 + 4 * par + nch]
            rmaxs = st(0 + 4 * par)
            rinvs = st(2 + 4 * par)
            negbs = st(3 + 4 * par)
            banks = []
            mm_vals = []
            for tc in range(nch):
                w = min(512, L - 512 * tc)
                bk = rot_sc[0] % 4
                rot_sc[0] += 1
                banks.append((bk, w))
                waits = list(bank_guard.get(bk, []))
                if idx == 0 and tc == 0:
                    waits.append(qTr_ready)
                    waits.extend(guards["kv"])
                    guards["kv"] = []
                mmv = P.t(lambda e, bk=bk, w=w, head=head, qt=qt, hkv=hkv, tc=tc:
                          e.matmul(bank(bk)[:, 0:w],
                                   qTr[:, 1024 * head + 128 * qt:1024 * head + 128 * (qt + 1)],
                                   kTr[:, 2048 * hkv + 512 * tc:2048 * hkv + 512 * tc + w],
                                   start=True, stop=True),
                          waits=waits, inc=PE)
                bank_ready[bk] = (PE, mmv)
                bank_guard[bk] = []
                mm_vals.append(mmv)
            bkd, wd = banks[-1]
            vv = P.v(lambda e, bkd=bkd, wd=wd: e.tensor_tensor(
                bank(bkd)[:, wd - 128:wd], bank(bkd)[:, wd - 128:wd], tri, op=ALU.add),
                waits=[(PE, mm_vals[-1])] + tabrdy[0], inc=VE)
            for tc in range(nch):
                bk, w = banks[tc]
                vv = P.v(lambda e, bk=bk, w=w, tc=tc, maxc=maxc: e.reduce_max(
                    maxc[:, tc:tc + 1], bank(bk)[:, 0:w], axis=AX.X),
                    waits=[(PE, mm_vals[tc])] if tc < nch - 1 else [(VE, vv)], inc=VE)
            vv = P.v(lambda e, maxc=maxc, rmaxs=rmaxs: e.reduce_max(rmaxs, maxc, axis=AX.X),
                     waits=[(VE, vv)], inc=VE)
            vneg = P.v(lambda e, rmaxs=rmaxs, negbs=negbs: e.tensor_scalar(
                negbs, rmaxs, -SCALE, None, op0=ALU.mult), waits=[(VE, vv)], inc=VE)
            flags_scr = []
            if flags["first_p"]:
                flags_scr = scr_ok
                flags["first_p"] = False
            aexp = 0
            for tc in range(nch):
                bk, w = banks[tc]
                ew = [(VE, vneg)] + (flags_scr if tc == 0 else [])
                if tc == 0 and idx - 2 in attval:
                    ew.append((PE, attval[idx - 2]))
                aexp = P.s(lambda e, par=par, tc=tc, w=w, bk=bk, negbs=negbs, sumc=sumc:
                           e.activation(p_sb[par][:, 512 * tc:512 * tc + w],
                                        bank(bk)[:, 0:w], AF.Exp,
                                        bias=negbs, scale=SCALE,
                                        accum_out=sumc[:, tc:tc + 1]),
                           waits=ew, inc=AC)
                bank_done(bk, AC, aexp)
            aexp = P.s(lambda e: e.copy(st(25), st(25)), waits=[(AC, aexp)], inc=AC)
            vv = P.v(lambda e, sumc=sumc: e.reduce_sum(st(1 + 4 * par), sumc, axis=AX.X),
                     waits=[(AC, aexp)], inc=VE)
            vrc = P.v(lambda e, rinvs=rinvs: e.reciprocal(rinvs, st(1 + 4 * par)),
                      waits=[(VE, vv)], inc=VE)
            vw = [(VE, vrc)]
            if idx - 2 in attval:
                vw.append((PE, attval[idx - 2]))
            vpb = P.v(lambda e, par=par, L=L, rinvs=rinvs: e.tensor_scalar(
                pb_sb[par][:, 0:L], p_sb[par][:, 0:L], rinvs, None, op0=ALU.mult),
                waits=vw, inc=VE)
            state[idx] = dict(par=par, hkv=hkv, qt=qt, head=head, L=L, vpb=vpb)

        def stage2(idx):
            stt_ = state.pop(idx)
            par, hkv, qt, head, L, vpb = (stt_["par"], stt_["hkv"], stt_["qt"],
                                          stt_["head"], stt_["L"], stt_["vpb"])
            ntb = L // 128
            ev_ac = ev_ve = 0
            for a in range((ntb + 3) // 4):
                tbk = 4 + (a % 2)
                n_in_batch = min(4, ntb - 4 * a)
                for j in range(n_in_batch):
                    tb = 4 * a + j
                    waits = []
                    if j == 0:
                        waits = list(bank_guard.get(("pt", tbk), []))
                        if a == 0:
                            waits.append((VE, vpb))
                            if idx == 0:
                                waits.append((PR, pr_val))
                    tv = P.t(lambda e, par=par, tb=tb, tbk=tbk, j=j:
                             e.transpose(bank_b16(tbk)[:, 128 * j:128 * (j + 1)],
                                         pb_sb[par][:, 128 * tb:128 * (tb + 1)], idb),
                             waits=waits, inc=PE if j == n_in_batch - 1 else None)
                if a % 2 == 0:
                    ev = P.s(lambda e, par=par, a=a, tbk=tbk, n=n_in_batch:
                             e.copy(pT_sb[par][:, 512 * a:512 * a + 128 * n],
                                    bank_b16(tbk)[:, 0:128 * n]),
                             waits=[(PE, tv)], inc=AC)
                    bank_guard[("pt", tbk)] = [(AC, ev)]
                    ev_ac = ev
                else:
                    ev = P.v(lambda e, par=par, a=a, tbk=tbk, n=n_in_batch:
                             e.tensor_copy(pT_sb[par][:, 512 * a:512 * a + 128 * n],
                                           bank_b16(tbk)[:, 0:128 * n]),
                             waits=[(PE, tv)], inc=VE)
                    bank_guard[("pt", tbk)] = [(VE, ev)]
                    ev_ve = ev
            pt_done = ([(AC, ev_ac)] if ev_ac else []) + ([(VE, ev_ve)] if ev_ve else [])
            abk = 6 + (idx % 2)
            mms = []
            for tb in range(ntb):
                def fn(e, tb=tb, par=par, hkv=hkv, abk=abk, ntb=ntb):
                    return e.matmul(bank(abk)[:, 0:128],
                                    v_tm[:, 512 * tb + 128 * hkv:512 * tb + 128 * (hkv + 1)],
                                    pT_sb[par][:, 128 * tb:128 * (tb + 1)],
                                    start=(tb == 0), stop=(tb == ntb - 1))
                mms.append(fn)
            attv = mm_group(abk, mms, extra_waits=pt_done)
            attval[idx] = attv
            ew = [bank_ready[abk]]
            if flags["first_att"]:
                ew.append((PE, a2a_last))
                flags["first_att"] = False
            aev = P.s(lambda e, head=head, qt=qt, abk=abk:
                      e.copy(attT[:, 1024 * head + 128 * qt:1024 * head + 128 * (qt + 1)],
                             bank(abk)[:, 0:128]),
                      waits=ew, inc=AC)
            bank_done(abk, AC, aev)
            return attv

        items = [(qt, head) for qt in range(8) for head in range(NH)]
        pend = None
        last_att_pe = 0
        for idx, (qt, head) in enumerate(items):
            stage1(idx, qt, head)
            if pend is not None:
                last_att_pe = stage2(pend)
            pend = idx
        last_att_pe = stage2(pend)
        return last_att_pe, AC.n

    def phase_a3(h, attT_AC, a2b_last):
        scrw = [(PE, a2b_last), (VE, VE.n)] + guards["scr"]
        guards["scr"] = []
        wo_val = [0] * NC_DM

        def load_wo(fo, waits=()):
            P.g(lambda e, fo=fo: e.dma_start(
                out=wo_sb[fo % 2], in_=wo[fo].rearrange("p c n -> p (c n)")),
                waits=waits, inc=wS[fo % 2], k=16)
            wo_val[fo] = wS[fo % 2].n

        load_wo(0, waits=guards["tab"])
        load_wo(1)
        guards["tab"] = []
        xr_rot = [0]
        xr_guard = [None, None]
        xr_v = [0, 0]
        first = True
        last_mm = 0
        last_add = 0
        for fo in range(NC_DM):
            for scol in range(2):
                jr = xr_rot[0] % 2
                xr_rot[0] += 1
                xw = [xr_guard[jr]] if xr_guard[jr] else []
                P.y(lambda e, fo=fo, scol=scol, jr=jr, h=h: e.dma_start(
                    out=xr_sb[jr],
                    in_=xT_r[fo][:, HALF * h + 512 * scol:HALF * h + 512 * (scol + 1)]),
                    waits=xw + (scrw if first else []), inc=xR[jr], k=16)
                xr_v[jr] = xR[jr].n
                xr_val = (xR[jr], xr_v[jr])
                bk = rot_sc[0] % 4
                rot_sc[0] += 1
                mms = []
                for c in range(NC_DM):
                    def fn(e, c=c, fo=fo, scol=scol, bk=bk):
                        return e.matmul(bank(bk), wo_sb[fo % 2][:, 128 * c:128 * (c + 1)],
                                        attT[:, 1024 * c + 512 * scol:1024 * c + 512 * (scol + 1)],
                                        start=(c == 0), stop=(c == NC_DM - 1))
                    mms.append(fn)
                ew = [(wS[fo % 2], wo_val[fo])] + ([(AC, attT_AC)] if scol == 0 else [])
                if first:
                    first = False
                last_mm = mm_group(bk, mms, extra_waits=ew)
                aw = [(PE, last_mm), xr_val]
                if fo == 0 and scol == 0:
                    aw += guards["acc"]
                    guards["acc"] = []
                last_add = P.v(lambda e, fo=fo, scol=scol, bk=bk, jr=jr: e.tensor_tensor(
                    acc[:, 1024 * fo + 512 * scol:1024 * fo + 512 * (scol + 1)],
                    bank(bk), xr_sb[jr], op=ALU.add),
                    waits=aw, inc=VE)
                bank_done(bk, VE, last_add)
                xr_guard[jr] = (VE, last_add)
            if fo + 2 < NC_DM:
                load_wo(fo + 2, waits=[(PE, last_mm)])
        if debug == "a3pre":
            dump(acc, 0, [(VE, last_add)])
            return None
        ydram_v = [0, 0]
        ln_last_tb = [0]
        prev_tb_pe = [a2b_last]
        ybf_first = [True]
        last_ybf_ve = [0]

        def layer_norm(h, write_back, a3_last_add):
            out_dma_val = [0, 0]
            for sch in range(8):
                tm = tmA if sch % 2 == 0 else tmB
                other = tmB if sch % 2 == 0 else tmA
                tin = 4 if sch % 2 == 0 else 6
                tout = 5 if sch % 2 == 0 else 7
                evv = 0
                for a in range(4):
                    for j in range(4):
                        f = 4 * a + j
                        waits = []
                        if j == 0:
                            waits = list(bank_guard.get(tin, []))
                            if a == 0 and sch == 0:
                                waits.append((VE, a3_last_add))
                            if a == 0 and sch >= 2:
                                waits.append((PE, ln_last_tb[0]))
                                waits.append((xR[sch % 2], ydram_v[sch % 2]))
                        tv = P.t(lambda e, f=f, sch=sch, tin=tin, j=j:
                                 e.transpose(bank(tin)[:, 128 * j:128 * (j + 1)],
                                             acc[:, 1024 * f + 128 * sch:1024 * f + 128 * (sch + 1)],
                                             idf),
                                 waits=waits, inc=PE if j == 3 else None)
                    evw = [(PE, tv)]
                    if sch == 0 and a == 0:
                        evw += scrw
                    evv = P.s(lambda e, a=a, tm=tm, tin=tin:
                              e.copy(tm[:, 512 * a:512 * (a + 1)], bank(tin)),
                              waits=evw, inc=AC)
                    bank_guard[tin] = [(AC, evv)]
                vs1 = P.v(lambda e, tm=tm: e.reduce_sum(st(16), tm, axis=AX.X),
                          waits=[(AC, evv)], inc=VE)
                sqw = [(PE, prev_tb_pe[0])] + (scrw if sch == 0 else [])
                if sch >= 1:
                    sqw.append((xR[(sch + 1) % 2], ydram_v[(sch + 1) % 2]))
                asq = P.s(lambda e, tm=tm, other=other: e.activation(
                    other, tm, AF.Square, accum_out=st(17)), waits=sqw, inc=AC)
                asq = P.s(lambda e: e.copy(st(25), st(25)), waits=[(AC, asq)], inc=AC)
                v1 = P.v(lambda e: e.tensor_scalar(st(18), st(16), 1.0 / DM, None, op0=ALU.mult),
                         waits=[(VE, vs1)], inc=VE)
                v2 = P.v(lambda e: e.tensor_scalar(st(19), st(17), 1.0 / DM, None, op0=ALU.mult),
                         waits=[(AC, asq), (VE, v1)], inc=VE)
                v3 = P.v(lambda e: e.tensor_tensor(st(20), st(18), st(18), op=ALU.mult),
                         waits=[(VE, v2)], inc=VE)
                vvar = P.v(lambda e: e.tensor_tensor(st(21), st(19), st(20), op=ALU.subtract),
                           waits=[(VE, v3)], inc=VE)
                P.s(lambda e: e.activation(st(22), st(21), AF.Ln, bias=st(24)),
                    waits=[(VE, vvar)])
                arstd = P.s(lambda e: e.activation(st(23), st(22), AF.Exp, scale=-0.5),
                            inc=AC)
                vnorm = P.v(lambda e, tm=tm: e.tensor_scalar(
                    tm, tm, st(18), st(23), op0=ALU.subtract, op1=ALU.mult),
                    waits=[(AC, arstd)], inc=VE)
                row0 = HALF * h + 128 * sch
                P.g(lambda e, row0=row0, tm=tm: e.dma_start(
                    out=y_dram[row0:row0 + 128, :], in_=tm),
                    waits=[(VE, vnorm)], inc=xR[sch % 2], k=16)
                P.y(lambda e, row0=row0, tm=tm: e.dma_start(
                    out=acc_dram[row0:row0 + 128, :], in_=tm),
                    waits=[(VE, vnorm)], inc=xR[sch % 2], k=16)
                ydram_v[sch % 2] = xR[sch % 2].n
                if debug == "ln1" and h == 0:
                    P.v(lambda e, sch=sch: e.tensor_copy(
                        wtok[:, 16 * sch:16 * sch + 9], stats[:, 16:25]),
                        waits=[(VE, vnorm)], inc=VE)
                if write_back:
                    for a in range(4):
                        for j in range(4):
                            f = 4 * a + j
                            waits = []
                            if j == 0:
                                waits = list(bank_guard.get(tout, []))
                                if a == 0:
                                    waits.append((VE, vnorm))
                            tv = P.t(lambda e, f=f, tm=tm, tout=tout, j=j:
                                     e.transpose(bank(tout)[:, 128 * j:128 * (j + 1)],
                                                 tm[:, 128 * f:128 * (f + 1)], idf),
                                     waits=waits, inc=PE if j == 3 else None)
                        ln_last_tb[0] = tv
                        accv = acc.rearrange("p (f c) -> p f c", f=NC_DM)
                        ybfv = ybf.rearrange("p (f c) -> p f c", f=NC_DM)
                        psv = bank(tout).rearrange("p (f c) -> p f c", f=4)
                        af = P.s(lambda e, a=a, sch=sch, accv=accv, psv=psv:
                                 e.copy(accv[:, 4 * a:4 * (a + 1), 128 * sch:128 * (sch + 1)], psv),
                                 waits=[(PE, tv)], inc=AC)
                        yw = [(AC, af)]
                        if ybf_first[0]:
                            yw += [(PE, a2b_last)]
                            ybf_first[0] = False
                        yv = P.v(lambda e, a=a, sch=sch, ybfv=ybfv, psv=psv:
                                 e.tensor_copy(ybfv[:, 4 * a:4 * (a + 1), 128 * sch:128 * (sch + 1)], psv),
                                 waits=yw, inc=VE)
                        last_ybf_ve[0] = yv
                        bank_guard[tout] = [(VE, yv)]
                    prev_tb_pe[0] = ln_last_tb[0]
            return last_ybf_ve[0]

        layer_norm(h, True, last_add)
        return last_mm, last_ybf_ve[0], ln_last_tb[0], layer_norm

    def phase_router(h, ybf_ready_ve):
        wtok_ve = []
        first = True
        for sch in range(8):
            mms = []
            for c in range(NC_DM):
                def fn(e, c=c, sch=sch):
                    return e.matmul(bank(0)[:, 0:32],
                                    ybf[:, 1024 * c + 128 * sch:1024 * c + 128 * (sch + 1)],
                                    rt_sb[:, 32 * c:32 * (c + 1)],
                                    start=(c == 0), stop=(c == NC_DM - 1))
                mms.append(fn)
            ew = [(VE, ybf_ready_ve), (PR, pr_val)] if first else []
            first = False
            mmv = mm_group(0, mms, extra_waits=ew)
            vm = P.v(lambda e: e.reduce_max(st(8), bank(0)[:, 0:32], axis=AX.X),
                     waits=[(PE, mmv)], inc=VE)
            vn = P.v(lambda e: e.tensor_scalar(st(9), st(8), -1.0, None, op0=ALU.mult),
                     waits=[(VE, vm)], inc=VE)
            ae = P.s(lambda e: e.activation(rp, bank(0)[:, 0:32], AF.Exp, bias=st(9)),
                     waits=[(VE, vn)], inc=AC)
            bank_done(0, AC, ae)
            bank_done(0, VE, vn)
            vq = P.v(lambda e: e.tensor_copy(rcur, rp), waits=[(AC, ae)], inc=VE)
            for r in range(4):
                vq = P.v(lambda e: e.reduce_max(st(10), rcur, axis=AX.X),
                         waits=[(VE, vq)], inc=VE)
                if r < 3:
                    vq = P.v(lambda e: e.tensor_scalar(rkeep, rcur, st(10), None, op0=ALU.is_ge),
                             waits=[(VE, vq)], inc=VE)
                    vq = P.v(lambda e: e.scalar_tensor_tensor(rcur, rkeep, -2.0, rcur,
                                                              op0=ALU.mult, op1=ALU.add),
                             waits=[(VE, vq)], inc=VE)
            vq = P.v(lambda e: e.tensor_scalar(rkeep, rp, st(10), None, op0=ALU.is_ge),
                     waits=[(VE, vq)], inc=VE)
            vq = P.v(lambda e: e.tensor_tensor(rp, rp, rkeep, op=ALU.mult),
                     waits=[(VE, vq)], inc=VE)
            vq = P.v(lambda e: e.reduce_sum(st(11), rp, axis=AX.X), waits=[(VE, vq)], inc=VE)
            vq = P.v(lambda e: e.reciprocal(st(12), st(11)), waits=[(VE, vq)], inc=VE)
            wv_ = P.v(lambda e, sch=sch: e.tensor_scalar(
                wtok[:, 32 * sch:32 * (sch + 1)], rp, st(12), None, op0=ALU.mult),
                waits=[(VE, vq)], inc=VE)
            wtok_ve.append(wv_)
            row0 = HALF * h + 128 * sch
            P.g(lambda e, sch=sch, row0=row0: e.dma_start(
                out=wtok_dram[row0:row0 + 128, :], in_=wtok[:, 32 * sch:32 * (sch + 1)]),
                waits=[(VE, wv_)], inc=sD, k=16)
        wtt_ac = 0
        for sch in range(8):
            waits = list(bank_guard.get(1, []))
            if sch == 0:
                waits.append((VE, wtok_ve[-1]))
            tv = P.t(lambda e, sch=sch: e.transpose(
                bank(1)[0:32, 0:128], wtok[:, 32 * sch:32 * (sch + 1)], idf),
                waits=waits, inc=PE)
            bank_ready[1] = (PE, tv)
            wtt_ac = P.s(lambda e, sch=sch: e.copy(
                wtokT[:, 128 * sch:128 * (sch + 1)], bank(1)[0:32, 0:128]),
                waits=[(PE, tv)], inc=AC)
            bank_guard[1] = [(AC, wtt_ac)]
        return wtt_ac

    sD_init = [0]

    def perm_init():
        itile = cv(P_ITL, 512, mybir.dt.int32)
        vi = P.v(lambda e: e.memset(itile, 10 ** 9), inc=VE)
        P.g(lambda e: e.dma_start(
            out=perm_dram[0:NE * CAP].rearrange("(p a) one -> p (a one)", p=128), in_=itile),
            waits=[(VE, vi)], inc=sD, k=16)
        sD_init[0] = sD.n

    def phase_idxbuild(h, wtt_ac):
        vk = P.v(lambda e: e.tensor_scalar(wtokT, wtokT, 0.0, None, op0=ALU.is_gt),
                 waits=[(AC, wtt_ac)], inc=VE)
        init = 0.0 if h == 0 else cntprev
        vscan = P.v(lambda e, init=init: e.tensor_tensor_scan(
            wtokT, wtokT, wtokT, init, op0=ALU.add, op1=ALU.bypass),
            waits=[(VE, vk)], inc=VE)
        vcnt = P.v(lambda e: e.tensor_copy(cntprev, wtokT[:, 1023:1024]),
                   waits=[(VE, vscan)], inc=VE)
        P.v(lambda e: e.tensor_scalar(st(15), iota_sb, float(NE * CAP), None, op0=ALU.add),
            waits=[(VE, vcnt)])
        scat_prev = [0, 0]
        pos_read = [0]
        for sc in range(8):
            scg = 8 * h + sc
            tbk = 6 + sc % 2
            tw = list(bank_guard.get(tbk, []))
            if sc == 0:
                tw.append((VE, vcnt))
            tv = P.t(lambda e, sc=sc, tbk=tbk: e.transpose(
                bank(tbk)[:, 0:32], wtokT[:, 128 * sc:128 * (sc + 1)], idf[0:32, 0:32]),
                waits=tw, inc=PE)
            bank_ready[tbk] = (PE, tv)
            pv = P.s(lambda e, tbk=tbk: e.copy(pos_tm, bank(tbk)[:, 0:32]),
                     waits=[(PE, tv), (VE, pos_read[0])], inc=AC)
            bank_guard[tbk] = [(AC, pv)]
            vv = P.v(lambda e, sc=sc: e.tensor_scalar(
                rcur, wtok[:, 32 * sc:32 * (sc + 1)], 0.0, None, op0=ALU.is_gt),
                waits=[(AC, pv)], inc=VE)
            vv = P.v(lambda e: e.tensor_scalar(ovm, pos_tm, CAP + 0.5, None, op0=ALU.is_le),
                     waits=[(VE, vv)], inc=VE)
            vv = P.v(lambda e: e.tensor_tensor(rcur, rcur, ovm, op=ALU.mult),
                     waits=[(VE, vv)], inc=VE)
            vv = P.v(lambda e: e.tensor_tensor(rp, pos_tm, ebase_sb, op=ALU.add),
                     waits=[(VE, vv)], inc=VE)
            pos_read[0] = vv
            vv = P.v(lambda e: e.tensor_scalar(rp, rp, -1.0, None, op0=ALU.add),
                     waits=[(VE, vv)], inc=VE)
            vv = P.v(lambda e: e.tensor_scalar(rkeep, rcur, -1.0e9, 1.0e9,
                                               op0=ALU.mult, op1=ALU.add),
                     waits=[(VE, vv)], inc=VE)
            vv = P.v(lambda e: e.tensor_tensor(rp, rp, rcur, op=ALU.mult),
                     waits=[(VE, vv)], inc=VE)
            vv = P.v(lambda e: e.tensor_tensor(rp, rp, rkeep, op=ALU.add),
                     waits=[(VE, vv)], inc=VE)
            vv = P.v(lambda e, scg=scg: e.tensor_scalar(
                st(9), iota_sb, float(128 * scg), None, op0=ALU.add),
                waits=[(VE, vv)], inc=VE)
            tkw = [(VE, vv)]
            if sc >= 2:
                tkw.append((sD, scat_prev[sc % 2]))
            vv = P.v(lambda e, sc=sc: e.tensor_copy(tok_i32[sc % 2], st(9)),
                     waits=tkw, inc=VE)
            for j in range(4):
                sl = 4 * (sc % 2) + j
                vv = P.v(lambda e: e.tensor_reduce(st(8), rp, axis=AX.X, op=ALU.min),
                         waits=[(VE, vv)], inc=VE)
                vv = P.v(lambda e: e.tensor_tensor(st(12), st(8), st(15), op=ALU.min),
                         waits=[(VE, vv)], inc=VE)
                vv = P.v(lambda e, sl=sl: e.tensor_copy(mj_i32[sl], st(12)),
                         waits=[(VE, vv)], inc=VE)
                vv = P.v(lambda e: e.tensor_scalar(rkeep, rp, st(8), None, op0=ALU.is_le),
                         waits=[(VE, vv)], inc=VE)
                vv = P.v(lambda e: e.scalar_tensor_tensor(rp, rkeep, 1.0e9, rp,
                                                          op0=ALU.mult, op1=ALU.add),
                         waits=[(VE, vv)], inc=VE)
                P.g(lambda e, sc=sc, sl=sl: e.indirect_dma_start(
                    perm_dram, IndirectOffsetOnAxis(mj_i32[sl], 0), tok_i32[sc % 2], None),
                    waits=[(VE, vv), (sD, sD_init[0])], inc=sD, k=16)
            scat_prev[sc % 2] = sD.n
        return sD.n

    def phase_sparse_b(bpool, ydram_rdy, wtok_rdy, perm_rdy, accdram_rdy, nexp=NE):
        NHF = NC_I // 2
        gA = [cv(12288 * sl_, 12288, BF16) for sl_ in range(3)]
        uA = [cv(36864 + 12288 * sl_, 12288, BF16) for sl_ in range(3)]
        dA = [cv(73728 + 12288 * ih, 12288, BF16) for ih in range(2)]
        y_gath = [cv(98304 + 16384 * par, 16384, BF16) for par in range(2)]
        yT_gath = cv(131072, 16384, BF16)
        hT2 = cv(147456, 6144, BF16)
        out_sb = [cv(153600 + 8192 * tp, 8192, F32) for tp in range(4)]
        wg_sb = [cv(186368 + 512 * par, 512, F32) for par in range(2)]
        idxi = [cv(187392 + 32 * par, 16, mybir.dt.int32) for par in range(2)]
        idxg = [cv(187456 + 32 * par, 16, mybir.dt.int32) for par in range(2)]
        idxa = [cv(187520 + 32 * par, 16, mybir.dt.int32) for par in range(2)]
        idxf = cv(187584, 16, F32)
        sg2 = cv(187648, 2048, F32)
        P.v(lambda e: e.tensor_scalar(st(11), iota_sb, float(S), None, op0=ALU.add))

        g_v = {}
        u_v = {}
        d_v = {}
        gu_guard = [[], [], []]
        dn_guard = [[], []]
        gath_guard = [[], []]
        idx_v = [0, 0]
        gath_v = [0, 0]
        scat_last = [0]

        def load_gu(e_, ih, waits=()):
            sl = slice(NHF * ih, NHF * (ih + 1))
            sb = (2 * e_ + ih) % 3
            P.g(lambda en, e_=e_, sl=sl, sb=sb: en.dma_start(
                out=gA[sb].rearrange("p (i cn) -> p i cn", i=NHF),
                in_=gate[e_, sl].rearrange("i p c n -> p i (c n)")),
                waits=list(waits) + gu_guard[sb], inc=wS[sb], k=16)
            g_v[(e_, ih)] = (wS[sb], wS[sb].n)
            P.g(lambda en, e_=e_, sl=sl, sb=sb: en.dma_start(
                out=uA[sb].rearrange("p (i cn) -> p i cn", i=NHF),
                in_=up[e_, sl].rearrange("i p c n -> p i (c n)")),
                inc=uS[sb], k=16)
            u_v[(e_, ih)] = (uS[sb], uS[sb].n)

        def load_dn(e_, ih, waits=()):
            sl = slice(NHF * ih, NHF * (ih + 1))
            P.g(lambda en, e_=e_, sl=sl, ih=ih: en.dma_start(
                out=dA[ih].rearrange("p (i f) -> p i f", i=NHF),
                in_=down[e_, sl].rearrange("i p f -> p i f")),
                waits=list(waits) + dn_guard[ih], inc=dS[ih], k=16)
            d_v[(e_, ih)] = dS[ih].n

        def load_gathers(e_, waits=()):
            par = e_ % 2
            for t in range(4):
                P.y(lambda en, e_=e_, t=t: en.dma_start(
                    out=idxi[par][:, t:t + 1],
                    in_=perm_dram[CAP * e_ + 128 * t:CAP * e_ + 128 * (t + 1)]),
                    waits=(list(waits) + gath_guard[par]) if t == 0 else [],
                    inc=xR[par], k=16)
            idx_v[par] = xR[par].n
            vv = P.v(lambda en, par=par: en.tensor_copy(idxf, idxi[par]),
                     waits=[(xR[par], idx_v[par])], inc=VE)
            vv = P.v(lambda en, par=par: en.tensor_scalar(idxa[par].bitcast(F32), idxf,
                                                          st(11), None, op0=ALU.min),
                     waits=[(VE, vv)], inc=VE)
            vv = P.v(lambda en, par=par: en.tensor_copy(idxa[par], idxa[par].bitcast(F32)),
                     waits=[(VE, vv)], inc=VE)
            vv = P.v(lambda en: en.tensor_scalar(idxf, idxf, float(S - 1), None,
                                                 op0=ALU.min), waits=[(VE, vv)], inc=VE)
            vv = P.v(lambda en, par=par: en.tensor_copy(idxg[par], idxf),
                     waits=[(VE, vv)], inc=VE)
            for t in range(4):
                P.g(lambda en, par=par, t=t: en.indirect_dma_start(
                    y_gath[par][:, 2048 * t:2048 * (t + 1)], None,
                    y_dram, IndirectOffsetOnAxis(idxg[par][:, t:t + 1], 0)),
                    waits=([(VE, vv)] + list(ydram_rdy) if t == 0 else []),
                    inc=OS[par], k=16)
            for t in range(4):
                P.g(lambda en, par=par, t=t: en.indirect_dma_start(
                    wg_sb[par][:, 32 * t:32 * (t + 1)], None,
                    wtok_dram, IndirectOffsetOnAxis(idxg[par][:, t:t + 1], 0)),
                    waits=(list(wtok_rdy) if t == 0 else []), inc=OS[par], k=16)
            gath_v[par] = OS[par].n

        first_w = [(sD, perm_rdy)]
        load_gathers(0, waits=first_w)
        load_gathers(1)
        load_gu(0, 0, waits=bpool)
        load_gu(0, 1)
        load_gu(1, 0)
        load_dn(0, 0)
        load_dn(0, 1)
        last_acc = 0
        for e_ in range(nexp):
            par = e_ % 2
            tpos = 0
            for t in range(4):
                for a in range(4):
                    tbk = (t * 4 + a) % 4
                    for j in range(4):
                        c = 4 * a + j
                        tw = []
                        if j == 0:
                            tw = list(bank_guard.get(tbk, []))
                            if t == 0 and a == 0:
                                tw += [(OS[par], gath_v[par])] + gu_guard[par]
                        tv = P.t(lambda en, par=par, t=t, c=c, tbk=tbk, j=j:
                                 en.transpose(bank_b16(tbk)[:, 128 * j:128 * (j + 1)],
                                              y_gath[par][:, 2048 * t + 128 * c:
                                                          2048 * t + 128 * (c + 1)], idb),
                                 waits=tw, inc=PE if j == 3 else None)
                    ytv = yT_gath.rearrange("p (c s) -> p c s", c=NC_DM)
                    psv = bank_b16(tbk)[:, 0:512].rearrange("p (c s) -> p c s", c=4)
                    ev = P.s(lambda en, par=par, t=t, a=a, ytv=ytv, psv=psv:
                             en.copy(ytv[:, 4 * a:4 * (a + 1), 128 * t:128 * (t + 1)], psv),
                             waits=[(PE, tv)], inc=AC)
                    bank_guard[tbk] = [(AC, ev)]
            tr_last = tv
            yT_rdy = AC.n
            last_h_ve = 0
            for ih in range(2):
                last_gu = 0
                for il in range(NHF):
                    i = NHF * ih + il
                    gbk = (i % 2) * 2
                    ubk = gbk + 1
                    mg = []
                    mu = []
                    sb = (2 * e_ + ih) % 3
                    for c in range(NC_DM):
                        def fg(en, c=c, il=il, sb=sb, par=par, gbk=gbk):
                            return en.matmul(bank(gbk),
                                             gA[sb][:, 2048 * il + 128 * c:2048 * il + 128 * (c + 1)],
                                             yT_gath[:, 512 * c:512 * (c + 1)],
                                             start=(c == 0), stop=(c == NC_DM - 1))
                        def fu(en, c=c, il=il, sb=sb, par=par, ubk=ubk):
                            return en.matmul(bank(ubk),
                                             uA[sb][:, 2048 * il + 128 * c:2048 * il + 128 * (c + 1)],
                                             yT_gath[:, 512 * c:512 * (c + 1)],
                                             start=(c == 0), stop=(c == NC_DM - 1))
                        mg.append(fg)
                        mu.append(fu)
                    ew = [g_v[(e_, ih)]]
                    if il == 0 and ih == 0:
                        ew.append((AC, yT_rdy))
                    gv = mm_group(gbk, mg, extra_waits=ew)
                    uv = mm_group(ubk, mu, extra_waits=[u_v[(e_, ih)]])
                    last_gu = uv
                    asil = P.s(lambda en, gbk=gbk: en.activation(sg2, bank(gbk), AF.Silu),
                               waits=[(PE, gv), (VE, last_h_ve)], inc=AC)
                    bank_done(gbk, AC, asil)
                    last_h_ve = P.v(lambda en, i=i, ubk=ubk: en.tensor_tensor(
                        hT2[:, 512 * i:512 * (i + 1)], sg2, bank(ubk), op=ALU.mult),
                        waits=[(PE, uv), (AC, asil)], inc=VE)
                    bank_done(ubk, VE, last_h_ve)
                gu_guard[(2 * e_ + ih) % 3] = [(PE, last_gu)]
                nh = 2 * e_ + ih + 3
                if nh // 2 < nexp:
                    load_gu(nh // 2, nh % 2, waits=[(PE, last_gu)])
            last_dn = 0
            wr_prev = scat_last[0]
            for t in range(4):
                ev_vals = []
                for fc in range(4):
                    bk = 4 + fc
                    mms = []
                    for ic in range(NC_I):
                        def fn(en, ic=ic, t=t, fc=fc, par=par, bk=bk):
                            return en.matmul(bank(bk),
                                             hT2[:, 512 * ic + 128 * t:512 * ic + 128 * (t + 1)],
                                             dA[ic // NHF][:, 2048 * (ic % NHF) + 512 * fc:
                                                           2048 * (ic % NHF) + 512 * (fc + 1)],
                                             start=(ic == 0), stop=(ic == NC_I - 1))
                        mms.append(fn)
                    ew = [(dS[0], d_v[(e_, 0)]), (dS[1], d_v[(e_, 1)])]
                    if t == 0 and fc == 0:
                        ew.append((VE, last_h_ve))
                    dv = mm_group(bk, mms, extra_waits=ew)
                    last_dn = dv
                    vw = [(PE, dv)] + ([(WR, wr_prev)] if fc == 0 else [])
                    vv = P.s(lambda en, t=t, fc=fc, par=par, bk=bk, e_=e_:
                             en.activation(out_sb[t][:, 512 * fc:512 * (fc + 1)],
                                           bank(bk), AF.Copy,
                                           scale=wg_sb[par][:, 32 * t + e_:32 * t + e_ + 1]),
                             waits=vw, inc=AC)
                    bank_done(bk, AC, vv)
                    ev_vals.append(vv)
                sw = [(AC, ev_vals[-1]), (WR, wr_prev)]
                if e_ == 0 and t == 0:
                    sw += list(accdram_rdy)
                P.g(lambda en, t=t, par=par: en.indirect_dma_start(
                    acc_dram, IndirectOffsetOnAxis(idxa[par][:, t:t + 1], 0),
                    out_sb[t], None, compute_op=ALU.add),
                    waits=sw, inc=WR, k=16)
            scat_last[0] = WR.n
            dn_guard[0] = [(PE, last_dn)]
            dn_guard[1] = [(PE, last_dn)]
            gath_guard[par] = [(PE, tr_last), (WR, scat_last[0])]
            if e_ + 2 < nexp:
                load_gathers(e_ + 2)
            if e_ + 1 < nexp:
                load_dn(e_ + 1, 0)
                load_dn(e_ + 1, 1)
        return scat_last[0], PE.n

    def phase_ln2_dram(scat_done, b_pe):
        tm2 = [cv(16384 * p_, 8192, F32) for p_ in range(2)]
        out_v = [0, 0]
        in_v = [0, 0]
        for sch in range(16):
            par = sch % 2
            tm = tm2[par]
            other = tm2[1 - par]
            iw = [(WR, scat_done), (PE, b_pe)] if sch < 2 else                  [(OS[par], out_v[par])]
            P.y(lambda e, sch=sch, tm=tm: e.dma_start(
                out=tm, in_=acc_dram[128 * sch:128 * (sch + 1), :]),
                waits=iw, inc=xR[par], k=16)
            in_v[par] = xR[par].n
            vs1 = P.v(lambda e, tm=tm: e.reduce_sum(st(16), tm, axis=AX.X),
                      waits=[(xR[par], in_v[par])], inc=VE)
            sqw = [(OS[1 - par], out_v[1 - par])] if sch >= 1 else [(PE, b_pe)]
            sqw += [(xR[par], in_v[par])]
            asq = P.s(lambda e, tm=tm, other=other: e.activation(
                other, tm, AF.Square, accum_out=st(17)), waits=sqw, inc=AC)
            asq = P.s(lambda e: e.copy(st(25), st(25)), waits=[(AC, asq)], inc=AC)
            v1 = P.v(lambda e: e.tensor_scalar(st(18), st(16), 1.0 / DM, None, op0=ALU.mult),
                     waits=[(VE, vs1)], inc=VE)
            v2 = P.v(lambda e: e.tensor_scalar(st(19), st(17), 1.0 / DM, None, op0=ALU.mult),
                     waits=[(AC, asq), (VE, v1)], inc=VE)
            v3 = P.v(lambda e: e.tensor_tensor(st(20), st(18), st(18), op=ALU.mult),
                     waits=[(VE, v2)], inc=VE)
            vvar = P.v(lambda e: e.tensor_tensor(st(21), st(19), st(20), op=ALU.subtract),
                       waits=[(VE, v3)], inc=VE)
            P.s(lambda e: e.activation(st(22), st(21), AF.Ln, bias=st(24)), waits=[(VE, vvar)])
            arstd = P.s(lambda e: e.activation(st(23), st(22), AF.Exp, scale=-0.5), inc=AC)
            vnorm = P.v(lambda e, tm=tm: e.tensor_scalar(
                tm, tm, st(18), st(23), op0=ALU.subtract, op1=ALU.mult),
                waits=[(AC, arstd)], inc=VE)
            P.y(lambda e, sch=sch, tm=tm: e.dma_start(out=outr[sch], in_=tm),
                waits=[(VE, vnorm)], inc=OS[par], k=16)
            out_v[par] = OS[par].n

    perm_init()
    a2a_last = phase_a2a(0)
    if debug == "qTr":
        b0 = dump(qTr, 0, [(VE, VE.n)])
        return finish_debug()
    if debug == "att00":
        phase_a2b(0, a2a_last)
        return finish_debug()
    a2b_last, attT_AC = phase_a2b(0, a2a_last)
    guards["tab"] = [(PE, a2b_last), (VE, VE.n)]
    if debug == "attT":
        b0 = dump(attT, 0, [(AC, attT_AC)])
        return finish_debug()
    if debug == "a3pre":
        phase_a3(0, attT_AC, a2b_last)
        return finish_debug()
    a3_last_mm, ybf_ve, ln_tb, layer_norm_fn = phase_a3(0, attT_AC, a2b_last)
    if debug == "ln1":
        b0 = dump(acc, 0, [(PE, ln_tb), (AC, AC.n), (VE, VE.n)])
        dump(ybf, b0, [(VE, ybf_ve)])
        return finish_debug()
    wtt_ac = phase_router(0, ybf_ve)
    if debug == "wtok":
        b0 = dump(wtok, 0, [(VE, VE.n)])
        P.g(lambda e: e.dma_start(out=outr[b0][0:32, 0:1024], in_=wtokT),
            waits=[(AC, wtt_ac)], inc=DG, k=16)
        return finish_debug()
    phase_idxbuild(0, wtt_ac)

    guards["xatt"] = [(PE, PE.n)]
    guards["ybf"] = [(PE, PE.n)]
    guards["acc"] = [(PE, PE.n)]
    guards["scr"] = [(PE, PE.n), (VE, VE.n)]
    P.y(lambda e: e.dma_start(out=cosf, in_=cosd), waits=[(PE, a2b_last)], inc=TB, k=16)
    P.y(lambda e: e.dma_start(out=sinf, in_=sind), inc=TB, k=16)
    guards["tab"] = guards["tab"] + [(TB, TB.n)]
    tabrdy[0] = [(TB, TB.n)]
    guards["kv"] = []

    a2a_last = phase_a2a(1)
    a2b_last, attT_AC = phase_a2b(1, a2a_last)
    guards["tab"] = [(PE, a2b_last), (VE, VE.n)]
    a3_last_mm, ybf_ve, ln_tb, _ = phase_a3(1, attT_AC, a2b_last)
    wtt_ac = phase_router(1, ybf_ve)
    perm_rdy = phase_idxbuild(1, wtt_ac)
    if debug == "perm":
        P.g(lambda e: e.dma_start(
            out=out[0:8, :], in_=perm_dram[0:NE * CAP].rearrange("(r c) one -> r (c one)", r=8)),
            waits=[(sD, perm_rdy)], inc=DG, k=16)
        P.g(lambda e: e.dma_start(out=out[8:40, 0:1024], in_=wtokT),
            waits=[(VE, VE.n)], inc=DG, k=16)
        return finish_debug()

    bpool = [(PE, PE.n), (AC, AC.n), (VE, VE.n),
             (xR[0], xR[0].n), (xR[1], xR[1].n)]
    ydram_rdy = [(xR[0], xR[0].n), (xR[1], xR[1].n)]
    wtok_rdy = [(sD, sD.n)]
    accdram_rdy = [(xR[0], xR[0].n), (xR[1], xR[1].n)]
    nexp = 1 if debug == "acc1" else NE
    scat_done, b_pe = phase_sparse_b(bpool, ydram_rdy, wtok_rdy, perm_rdy,
                                     accdram_rdy, nexp=nexp)
    if debug == "acc1":
        for a_ in range(16):
            P.g(lambda e, a_=a_: e.dma_start(
                out=outr[a_], in_=acc_dram[128 * a_:128 * (a_ + 1), :]),
                waits=[(WR, scat_done)] if a_ == 0 else [], inc=DG, k=16)
        return finish_debug()
    phase_ln2_dram(scat_done, b_pe)

    P.y(lambda e: e.nop(), waits=[(OS[0], OS[0].n), (OS[1], OS[1].n)])
    P.finalize()
    return nc



def _host_prep(L0_attn_x, L0_attn_Wqkv, L0_attn_Wo, L0_ffn_router_w,
               L0_ffn_gate_w, L0_ffn_up_w, L0_ffn_down_w):
    f32 = np.float32
    Wqkv = np.asarray(L0_attn_Wqkv, f32)
    Wo = np.asarray(L0_attn_Wo, f32)
    router = np.asarray(L0_ffn_router_w, f32)
    gate_w = np.asarray(L0_ffn_gate_w, f32)
    up_w = np.asarray(L0_ffn_up_w, f32)
    down_w = np.asarray(L0_ffn_down_w, f32)

    com = {}
    wq_m = Wqkv[:, :NH * HD]
    com["wq"] = np.ascontiguousarray(
        wq_m.reshape(NC_DM, 128, NH, 128).transpose(2, 1, 0, 3))
    wk_m = Wqkv[:, NH * HD:(NH + NKV) * HD]
    com["wk"] = np.ascontiguousarray(
        wk_m.reshape(NC_DM, 128, NKV, 128).transpose(2, 1, 0, 3))
    wv_m = Wqkv[:, (NH + NKV) * HD:]
    com["wv"] = np.ascontiguousarray(
        wv_m.reshape(NC_DM, 128, NKV * 128).transpose(1, 0, 2))
    com["wo"] = np.ascontiguousarray(
        Wo.reshape(NC_DM, 128, NC_DM, 128).transpose(2, 1, 0, 3))
    com["rt"] = np.ascontiguousarray(
        router.T.reshape(NC_DM, 128, NE).transpose(1, 0, 2))
    com["gate"] = np.ascontiguousarray(
        gate_w.reshape(NE, NC_I, 128, NC_DM, 128).transpose(0, 1, 4, 3, 2))
    com["up"] = np.ascontiguousarray(
        up_w.reshape(NE, NC_I, 128, NC_DM, 128).transpose(0, 1, 4, 3, 2))
    com["down"] = np.ascontiguousarray(
        down_w.reshape(NE, DM, NC_I, 128).transpose(0, 2, 3, 1))

    j = np.arange(64, dtype=np.float64)
    freqs = 1.0 / THETA ** (2.0 * j / HD)
    ang = np.arange(S, dtype=np.float64)[None, :] * freqs[:, None]
    cos64 = np.cos(ang).astype(f32)
    sin64 = np.sin(ang).astype(f32)
    com["cosd"] = np.concatenate([cos64, cos64], axis=0)
    com["sind"] = np.concatenate([-sin64, sin64], axis=0)
    qi = np.arange(128)[:, None]
    tj = np.arange(128)[None, :]
    com["trid"] = np.where(tj > qi, np.float32(MASKNEG), np.float32(0.0)).astype(f32)
    com["identd"] = np.eye(128, dtype=f32)
    com["iotad"] = np.arange(128, dtype=f32)[:, None]
    com["ebased"] = np.broadcast_to(
        (np.arange(NE, dtype=f32) * CAP)[None, :], (128, NE)).copy()

    x = np.asarray(L0_attn_x, f32)
    xTs = [np.ascontiguousarray(x[b].T) for b in range(B)]
    return com, xTs


def kernel(L0_attn_x, L0_attn_Wqkv, L0_attn_Wo, L0_ffn_router_w,
           L0_ffn_gate_w, L0_ffn_up_w, L0_ffn_down_w):
    com, xTs = _host_prep(L0_attn_x, L0_attn_Wqkv, L0_attn_Wo, L0_ffn_router_w,
                          L0_ffn_gate_w, L0_ffn_up_w, L0_ffn_down_w)
    nc = build_program()
    in_maps = [dict(com, xT=xTs[b]) for b in range(B)]
    res = run_bass_kernel_spmd(nc, in_maps, core_ids=list(range(B)))
    return np.stack([r["out"] for r in res.results], axis=0)


# version 10 (final)
# speedup vs baseline: 1.0587x; 1.0333x over previous
import math
import numpy as np

import concourse.bass as bass
import concourse.mybir as mybir
from concourse.bass_utils import run_bass_kernel_spmd
from concourse.bass_core_rust import IndirectOffsetOnAxis

F32 = mybir.dt.float32
BF16 = mybir.dt.bfloat16
U8 = mybir.dt.uint8
AF = mybir.ActivationFunctionType
ALU = mybir.AluOpType
AX = mybir.AxisListType

B = 8
S = 2048
DM = 2048
NH = 16
NKV = 4
HD = 128
GQ = NH // NKV
NE = 32
DI = 768
TOPK = 4
HALF = S // 2
NC_DM = DM // 128
NC_I = DI // 128
SCALE = 1.0 / math.sqrt(HD)
EPS = 1e-5
MASKNEG = -1.0e9
THETA = 10000.0

O_ACC = 0
O_YBF = 65536
O_SCR = 98304
O_XATT = 118784
O_KV = 151552
O_TAB = 184320
O_P = 200704
P_IDB = O_P
P_IDF = O_P + 256
P_TRI = O_P + 768
P_ONES = O_P + 1280
P_WTOK = O_P + 1792
P_WTT = O_P + 2816
P_ST = O_P + 6912
P_RT = O_P + 7936
P_I32 = O_P + 8960
P_ITL = O_P + 9024
ARENA_BYTES = O_P + 9536
CAP = 512


class Sem:
    def __init__(self, h, name):
        self.h = h
        self.name = name
        self.n = 0




class Prog:
    ENGINES = ("tensor", "vector", "scalar", "gpsimd", "sync")

    def __init__(self, nc):
        self.nc = nc
        self.stream = {e: [] for e in self.ENGINES}

    def sem(self, name):
        return Sem(self.nc.alloc_semaphore(name), name)

    def op(self, eng, fn, waits=(), inc=None, k=1):
        for (s, v) in waits:
            assert v <= s.n, f"wait on future value {v}>{s.n} of {s.name}"
        val = None
        if inc is not None:
            inc.n += k
            val = inc.n
        self.stream[eng].append((fn, tuple(waits), inc, k))
        return val

    def t(self, fn, waits=(), inc=None, k=1):
        return self.op("tensor", fn, waits, inc, k)

    def v(self, fn, waits=(), inc=None, k=1):
        return self.op("vector", fn, waits, inc, k)

    def s(self, fn, waits=(), inc=None, k=1):
        return self.op("scalar", fn, waits, inc, k)

    def g(self, fn, waits=(), inc=None, k=1):
        return self.op("gpsimd", fn, waits, inc, k)

    def y(self, fn, waits=(), inc=None, k=1):
        return self.op("sync", fn, waits, inc, k)

    def finalize(self):
        with self.nc.Block() as block:
            for ename in self.ENGINES:
                ops = self.stream[ename]
                if not ops:
                    continue

                def make(ops_):
                    def emit(eng):
                        for fn, waits, inc, k in ops_:
                            for (sm, val) in waits:
                                eng.wait_ge(sm.h, val)
                            instr = fn(eng)
                            if inc is not None:
                                instr.then_inc(inc.h, k)
                    return emit
                getattr(block, ename)(make(ops))


def build_program(debug=None):
    nc = bass.Bass("TRN2", target_bir_lowering=False)

    xT = nc.dram_tensor("xT", [DM, S], F32, kind="ExternalInput").ap()
    wq = nc.dram_tensor("wq", [NH, 128, NC_DM, 128], F32, kind="ExternalInput").ap()
    wk = nc.dram_tensor("wk", [NKV, 128, NC_DM, 128], F32, kind="ExternalInput").ap()
    wv = nc.dram_tensor("wv", [128, NC_DM, NKV * 128], F32, kind="ExternalInput").ap()
    wo = nc.dram_tensor("wo", [NC_DM, 128, NC_DM, 128], F32, kind="ExternalInput").ap()
    rt = nc.dram_tensor("rt", [128, NC_DM, NE], F32, kind="ExternalInput").ap()
    gate = nc.dram_tensor("gate", [NE, NC_I, 128, NC_DM, 128], F32, kind="ExternalInput").ap()
    up = nc.dram_tensor("up", [NE, NC_I, 128, NC_DM, 128], F32, kind="ExternalInput").ap()
    down = nc.dram_tensor("down", [NE, NC_I, 128, DM], F32, kind="ExternalInput").ap()
    cosd = nc.dram_tensor("cosd", [128, S], F32, kind="ExternalInput").ap()
    sind = nc.dram_tensor("sind", [128, S], F32, kind="ExternalInput").ap()
    trid = nc.dram_tensor("trid", [128, 128], F32, kind="ExternalInput").ap()
    identd = nc.dram_tensor("identd", [128, 128], F32, kind="ExternalInput").ap()
    iotad = nc.dram_tensor("iotad", [128, 1], F32, kind="ExternalInput").ap()
    ebased = nc.dram_tensor("ebased", [128, 32], F32, kind="ExternalInput").ap()
    out = nc.dram_tensor("out", [S, DM], F32, kind="ExternalOutput").ap()
    y_dram = nc.dram_tensor("y_dram", [S, DM], BF16, kind="Internal").ap()
    acc_dram = nc.dram_tensor("acc_dram", [S + 128, DM], F32, kind="Internal").ap()
    wtok_dram = nc.dram_tensor("wtok_dram", [S, NE], F32, kind="Internal").ap()
    perm_dram = nc.dram_tensor("perm_dram", [NE * CAP + 128, 1], mybir.dt.int32, kind="Internal").ap()

    arena = nc.alloc_sbuf_tensor("arena", [128, ARENA_BYTES], U8).ap()

    def cv(off, nbytes, dtype):
        return arena[:, off:off + nbytes].bitcast(dtype)

    idb = cv(P_IDB, 256, BF16)
    idf = cv(P_IDF, 512, F32)
    tri = cv(P_TRI, 512, F32)
    ones_r = cv(P_ONES, 512, F32)[0:1, :]
    wtok = cv(P_WTOK, 1024, F32)
    wtokT = cv(P_WTT, 4096, F32)[0:32, :]
    stats = cv(P_ST, 1024, F32)

    def st(i):
        return stats[:, i:i + 1]
    rp = stats[:, 32:64]
    rcur = stats[:, 64:96]
    rkeep = stats[:, 96:128]
    ovm = stats[:, 128:160]
    pos_tm = stats[:, 160:192]
    ebase_sb = stats[:, 192:224]
    iota_sb = stats[:, 14:15]
    cntprev = stats[0:32, 13:14]
    tok_i32 = [cv(P_I32 + 4 * j, 4, mybir.dt.int32) for j in range(2)]
    mj_i32 = [cv(P_I32 + 8 + 4 * j, 4, mybir.dt.int32) for j in range(8)]

    acc = cv(O_ACC, 65536, F32)
    ybf = cv(O_YBF, 32768, BF16)
    kTr = cv(O_KV, 16384, BF16)
    v_tm = cv(O_KV + 16384, 16384, BF16)
    cosf = cv(O_TAB, 8192, F32)
    sinf = cv(O_TAB + 8192, 8192, F32)
    xatt = cv(O_XATT, 32768, BF16)

    PS = nc.alloc_psum_tensor("PS", [128, 4096], F32).ap()

    def bank(i, n=1):
        return PS[:, 512 * i:512 * (i + n)]

    def bank_b16(i):
        return PS[:, 512 * i:512 * (i + 1)].bitcast(BF16)

    P = Prog(nc)
    PR = P.sem("PR")
    TB = P.sem("TB")
    wS = [P.sem("wA"), P.sem("wB"), P.sem("wC")]
    uS = [P.sem("uA"), P.sem("uB"), P.sem("uC")]
    dS = [P.sem("dA"), P.sem("dB")]
    xH = P.sem("xH")
    xR = [P.sem("xRA"), P.sem("xRB")]
    WR = P.sem("WR")
    sD = P.sem("sD")
    OS = [P.sem("OA"), P.sem("OB")]
    DG = P.sem("DG")
    PO = P.sem("PO")
    PE = P.sem("PE")
    AC = P.sem("AC")
    VE = P.sem("VE")

    bank_guard = {}
    bank_ready = {}

    def mm_group(bk, mms, extra_waits=()):
        n = len(mms)
        for i, fn in enumerate(mms):
            waits = []
            if i == 0:
                waits.extend(bank_guard.get(bk, []))
                waits.extend(extra_waits)
            val = P.t(fn, waits=waits, inc=PE if i == n - 1 else None)
        bank_ready[bk] = (PE, val)
        bank_guard[bk] = []
        return val

    def bank_done(bk, sem, val):
        bank_guard.setdefault(bk, []).append((sem, val))

    P.y(lambda e: e.dma_start(out=cosf, in_=cosd), inc=TB, k=16)
    P.y(lambda e: e.dma_start(out=sinf, in_=sind), inc=TB, k=16)
    P.y(lambda e: e.dma_start(out=tri, in_=trid), inc=TB, k=16)
    P.y(lambda e: e.dma_start(out=idf, in_=identd), inc=TB, k=16)
    tabrdy = [[(TB, TB.n)]]
    P.g(lambda e: e.dma_start(out=idb, in_=identd), inc=PR, k=16)
    eps_ve = P.v(lambda e: e.memset(st(24), EPS), inc=VE)

    def xfull_chunk(c):
        if c < 8:
            return xatt[:, 2048 * c:2048 * (c + 1)]
        return ybf[:, 2048 * (c - 8):2048 * (c - 7)]

    xT_r = xT.rearrange("(c p) s -> c p s", p=128)
    for c in range(NC_DM):
        P.g(lambda e, c=c: e.dma_start(out=xfull_chunk(c), in_=xT_r[c]), inc=PR, k=16)

    wv_sb = cv(O_ACC, 16384, BF16)
    wk_sb = [cv(O_ACC + 16384 + 4096 * j, 4096, BF16) for j in range(2)]
    P.g(lambda e: e.dma_start(out=wv_sb, in_=wv.rearrange("p c n -> p (c n)")), inc=PR, k=16)
    rt_sb = cv(P_RT, 1024, BF16)
    P.g(lambda e: e.dma_start(out=rt_sb, in_=rt.rearrange("p c n -> p (c n)")), inc=PR, k=16)
    P.g(lambda e: e.dma_start(out=iota_sb, in_=iotad), inc=PR, k=16)
    P.g(lambda e: e.dma_start(out=ebase_sb, in_=ebased), inc=PR, k=16)
    pr_val = PR.n
    wk_val = [0, 0, 0, 0]

    def load_wk(kf, waits=()):
        P.g(lambda e, kf=kf: e.dma_start(out=wk_sb[kf % 2],
                                         in_=wk[kf].rearrange("p c n -> p (c n)")),
            waits=waits, inc=wS[kf % 2], k=16)
        wk_val[kf] = wS[kf % 2].n

    load_wk(0)
    load_wk(1)

    rot = 0
    for sc in range(16):
        bk = rot % 4
        rot += 1
        mms = []
        for c in range(NC_DM):
            def fn(e, c=c, sc=sc, bk=bk):
                return e.matmul(bank(bk), xfull_chunk(c)[:, 128 * sc:128 * (sc + 1)],
                                wv_sb[:, 512 * c:512 * (c + 1)],
                                start=(c == 0), stop=(c == NC_DM - 1))
            mms.append(fn)
        w = [(PR, pr_val)] if sc == 0 else []
        val = mm_group(bk, mms, extra_waits=w)
        av = P.s(lambda e, sc=sc, bk=bk: e.copy(v_tm[:, 512 * sc:512 * (sc + 1)], bank(bk)),
                 waits=[bank_ready[bk]], inc=AC)
        bank_done(bk, AC, av)
    v_done_AC = AC.n

    swp = cv(O_SCR, 2048, F32)
    t2s = cv(O_SCR + 2048, 2048, F32)
    t3s = cv(O_SCR + 4096, 2048, F32)
    rope_guard = [None]

    def rope_apply(bk, col0, dst):
        waits = [bank_ready[bk]] + ([rope_guard[0]] if rope_guard[0] else [])
        a1 = P.s(lambda e, bk=bk: e.copy(swp[0:64, :], bank(bk)[64:128, :]), waits=waits, inc=AC)
        a2 = P.s(lambda e, bk=bk: e.copy(swp[64:128, :], bank(bk)[0:64, :]), inc=AC)
        colr = slice(col0, col0 + 512)
        v1 = P.v(lambda e, bk=bk, colr=colr: e.tensor_tensor(t2s, bank(bk), cosf[:, colr], op=ALU.mult),
                 waits=[(AC, a2)] + tabrdy[0], inc=VE)
        P.v(lambda e, colr=colr: e.tensor_tensor(t3s, swp, sinf[:, colr], op=ALU.mult),
            waits=[(AC, a2)])
        v3 = P.v(lambda e, dst=dst: e.tensor_tensor(dst, t2s, t3s, op=ALU.add), inc=VE)
        rope_guard[0] = (VE, v3)
        bank_done(bk, VE, v1)
        bank_done(bk, AC, a2)
        return v3

    for kf in range(NKV):
        for scol in range(4):
            bk = rot % 4
            rot += 1
            mms = []
            for c in range(NC_DM):
                def fn(e, c=c, kf=kf, scol=scol, bk=bk):
                    return e.matmul(bank(bk), wk_sb[kf % 2][:, 128 * c:128 * (c + 1)],
                                    xfull_chunk(c)[:, 512 * scol:512 * (scol + 1)],
                                    start=(c == 0), stop=(c == NC_DM - 1))
                mms.append(fn)
            val = mm_group(bk, mms, extra_waits=[(wS[kf % 2], wk_val[kf])])
            if scol == 3 and kf + 2 < NKV:
                load_wk(kf + 2, waits=[(PE, val)])
            dst = kTr[:, 2048 * kf + 512 * scol: 2048 * kf + 512 * (scol + 1)]
            rope_apply(bk, 512 * scol, dst)
    kv_done = (VE, VE.n)

    outr = out.rearrange("(a p) n -> a p n", p=128)

    def dump(ap, blk0, waits):
        per = 2048
        ncols = ap.shape[1]
        nblk = (ncols + per - 1) // per
        for a in range(nblk):
            w = waits if a == 0 else []
            c0, c1 = per * a, min(per * (a + 1), ncols)
            P.g(lambda e, a=a, c0=c0, c1=c1: e.dma_start(
                out=outr[blk0 + a][:, 0:c1 - c0], in_=ap[:, c0:c1]),
                waits=w, inc=DG, k=16)
        return blk0 + nblk

    def finish_debug():
        fin = DG.n
        P.y(lambda e: e.nop(), waits=[(DG, fin)])
        P.finalize()
        return nc

    if debug == "kv":
        b0 = dump(kTr, 0, [kv_done, (AC, v_done_AC)])
        b0 = dump(v_tm, b0, [])
        dump(cosf, b0, tabrdy[0])
        return finish_debug()

    guards = {
        "xatt": [(PE, PE.n)],
        "ybf": [(PE, PE.n)],
        "acc": [(PE, PE.n)],
        "kv": [],
        "tab": [],
        "scr": [kv_done],
    }

    def xh_chunk(c):
        return xatt[:, 1024 * c:1024 * (c + 1)]

    qTr = ybf
    attT = xatt

    wq_sb = [cv(O_SCR + 6144 + 4096 * j, 4096, BF16) for j in range(2)]
    p_sb = [cv(O_SCR, 8192, F32), cv(O_SCR + 8192, 8192, F32)]
    pT_sb = [cv(O_SCR + 4096, 4096, BF16), cv(O_SCR + 8192 + 4096, 4096, BF16)]
    pb_sb = [cv(O_TAB + 4096 * j, 4096, BF16) for j in range(2)]
    xr_sb = [cv(O_SCR + 2048 * j, 2048, F32) for j in range(2)]
    wo_sb = [cv(O_TAB + 4096 * j, 4096, BF16) for j in range(2)]
    tmA = cv(O_SCR + 4096, 8192, F32)
    tmB = cv(O_SCR + 12288, 8192, F32)
    gA = [cv(O_XATT + 12288 * j, 12288, BF16) for j in range(2)]
    uA = [cv(O_XATT + 24576 + 12288 * j, 12288, BF16) for j in range(2)]
    dA = [cv(O_XATT + 49152 + 12288 * j, 12288, BF16) for j in range(2)]
    wrow = cv(O_XATT + 73728, 4096, F32)[0:1, :]
    sg_sb = cv(O_SCR, 2048, F32)
    uw_sb = cv(O_SCR + 2048, 2048, F32)
    w_sb = cv(O_SCR + 4096, 4096, F32)
    hT = cv(O_SCR + 8192, 12288, BF16)

    rot_sc = [0]

    def phase_a2a(h):
        for c in range(NC_DM):
            P.g(lambda e, c=c: e.dma_start(
                out=xh_chunk(c), in_=xT_r[c][:, HALF * h:HALF * (h + 1)]),
                waits=guards["xatt"] if c == 0 else [], inc=xH, k=16)
        xh_val = xH.n
        guards["xatt"] = []
        rope_guard[0] = (VE, VE.n)

        wq_val = [0] * NH

        def load_wq(hc, waits=()):
            P.g(lambda e, hc=hc: e.dma_start(
                out=wq_sb[hc % 2], in_=wq[hc].rearrange("p c n -> p (c n)")),
                waits=waits, inc=wS[hc % 2], k=16)
            wq_val[hc] = wS[hc % 2].n

        load_wq(0, waits=guards["scr"])
        load_wq(1)
        first = True
        for hc in range(NH):
            last_mm = 0
            for scol in range(2):
                bk = rot_sc[0] % 4
                rot_sc[0] += 1
                mms = []
                for c in range(NC_DM):
                    def fn(e, c=c, hc=hc, scol=scol, bk=bk):
                        return e.matmul(bank(bk), wq_sb[hc % 2][:, 128 * c:128 * (c + 1)],
                                        xh_chunk(c)[:, 512 * scol:512 * (scol + 1)],
                                        start=(c == 0), stop=(c == NC_DM - 1))
                    mms.append(fn)
                ew = [(wS[hc % 2], wq_val[hc]), (xH, xh_val)]
                if first:
                    ew += guards["ybf"]
                    first = False
                last_mm = mm_group(bk, mms, extra_waits=ew)
                dst = qTr[:, 1024 * hc + 512 * scol:1024 * hc + 512 * (scol + 1)]
                rope_apply(bk, HALF * h + 512 * scol, dst)
            if hc + 2 < NH:
                load_wq(hc + 2, waits=[(PE, last_mm)])
        guards["ybf"] = []
        return last_mm

    def phase_a2b(h, a2a_last):
        qTr_ready = (VE, VE.n)
        scr_ok = [(PE, a2a_last), (VE, VE.n)] + guards["scr"]
        state = {}
        attval = {}
        flags = {"first_p": True, "first_att": True}

        def stage1(idx, qt, head):
            par = idx % 2
            hkv = head // GQ
            qglob = 8 * h + qt
            L = 128 * (qglob + 1)
            nch = (L + 511) // 512
            maxc = stats[:, 232 + 8 * par:232 + 8 * par + nch]
            sumc = stats[:, 224 + 4 * par:224 + 4 * par + nch]
            rmaxs = st(0 + 4 * par)
            rinvs = st(2 + 4 * par)
            negbs = st(3 + 4 * par)
            banks = []
            mm_vals = []
            for tc in range(nch):
                w = min(512, L - 512 * tc)
                bk = rot_sc[0] % 4
                rot_sc[0] += 1
                banks.append((bk, w))
                waits = list(bank_guard.get(bk, []))
                if idx == 0 and tc == 0:
                    waits.append(qTr_ready)
                    waits.extend(guards["kv"])
                    guards["kv"] = []
                mmv = P.t(lambda e, bk=bk, w=w, head=head, qt=qt, hkv=hkv, tc=tc:
                          e.matmul(bank(bk)[:, 0:w],
                                   qTr[:, 1024 * head + 128 * qt:1024 * head + 128 * (qt + 1)],
                                   kTr[:, 2048 * hkv + 512 * tc:2048 * hkv + 512 * tc + w],
                                   start=True, stop=True),
                          waits=waits, inc=PE)
                bank_ready[bk] = (PE, mmv)
                bank_guard[bk] = []
                mm_vals.append(mmv)
            bkd, wd = banks[-1]
            vv = P.v(lambda e, bkd=bkd, wd=wd: e.tensor_tensor(
                bank(bkd)[:, wd - 128:wd], bank(bkd)[:, wd - 128:wd], tri, op=ALU.add),
                waits=[(PE, mm_vals[-1])] + tabrdy[0], inc=VE)
            for tc in range(nch):
                bk, w = banks[tc]
                vv = P.v(lambda e, bk=bk, w=w, tc=tc, maxc=maxc: e.reduce_max(
                    maxc[:, tc:tc + 1], bank(bk)[:, 0:w], axis=AX.X),
                    waits=[(PE, mm_vals[tc])] if tc < nch - 1 else [(VE, vv)], inc=VE)
            vv = P.v(lambda e, maxc=maxc, rmaxs=rmaxs: e.reduce_max(rmaxs, maxc, axis=AX.X),
                     waits=[(VE, vv)], inc=VE)
            vneg = P.v(lambda e, rmaxs=rmaxs, negbs=negbs: e.tensor_scalar(
                negbs, rmaxs, -SCALE, None, op0=ALU.mult), waits=[(VE, vv)], inc=VE)
            flags_scr = []
            if flags["first_p"]:
                flags_scr = scr_ok
                flags["first_p"] = False
            aexp = 0
            for tc in range(nch):
                bk, w = banks[tc]
                ew = [(VE, vneg)] + (flags_scr if tc == 0 else [])
                if tc == 0 and idx - 2 in attval:
                    ew.append((PE, attval[idx - 2]))
                aexp = P.s(lambda e, par=par, tc=tc, w=w, bk=bk, negbs=negbs, sumc=sumc:
                           e.activation(p_sb[par][:, 512 * tc:512 * tc + w],
                                        bank(bk)[:, 0:w], AF.Exp,
                                        bias=negbs, scale=SCALE,
                                        accum_out=sumc[:, tc:tc + 1]),
                           waits=ew, inc=AC)
                bank_done(bk, AC, aexp)
            aexp = P.s(lambda e: e.copy(st(25), st(25)), waits=[(AC, aexp)], inc=AC)
            vv = P.v(lambda e, sumc=sumc: e.reduce_sum(st(1 + 4 * par), sumc, axis=AX.X),
                     waits=[(AC, aexp)], inc=VE)
            vrc = P.v(lambda e, rinvs=rinvs: e.reciprocal(rinvs, st(1 + 4 * par)),
                      waits=[(VE, vv)], inc=VE)
            vw = [(VE, vrc)]
            if idx - 2 in attval:
                vw.append((PE, attval[idx - 2]))
            vpb = P.v(lambda e, par=par, L=L, rinvs=rinvs: e.tensor_scalar(
                pb_sb[par][:, 0:L], p_sb[par][:, 0:L], rinvs, None, op0=ALU.mult),
                waits=vw, inc=VE)
            state[idx] = dict(par=par, hkv=hkv, qt=qt, head=head, L=L, vpb=vpb)

        def stage2(idx):
            stt_ = state.pop(idx)
            par, hkv, qt, head, L, vpb = (stt_["par"], stt_["hkv"], stt_["qt"],
                                          stt_["head"], stt_["L"], stt_["vpb"])
            ntb = L // 128
            ev_ac = ev_ve = 0
            for a in range((ntb + 3) // 4):
                tbk = 4 + (a % 2)
                n_in_batch = min(4, ntb - 4 * a)
                for j in range(n_in_batch):
                    tb = 4 * a + j
                    waits = []
                    if j == 0:
                        waits = list(bank_guard.get(("pt", tbk), []))
                        if a == 0:
                            waits.append((VE, vpb))
                            if idx == 0:
                                waits.append((PR, pr_val))
                    tv = P.t(lambda e, par=par, tb=tb, tbk=tbk, j=j:
                             e.transpose(bank_b16(tbk)[:, 128 * j:128 * (j + 1)],
                                         pb_sb[par][:, 128 * tb:128 * (tb + 1)], idb),
                             waits=waits, inc=PE if j == n_in_batch - 1 else None)
                if a % 2 == 0:
                    ev = P.s(lambda e, par=par, a=a, tbk=tbk, n=n_in_batch:
                             e.copy(pT_sb[par][:, 512 * a:512 * a + 128 * n],
                                    bank_b16(tbk)[:, 0:128 * n]),
                             waits=[(PE, tv)], inc=AC)
                    bank_guard[("pt", tbk)] = [(AC, ev)]
                    ev_ac = ev
                else:
                    ev = P.v(lambda e, par=par, a=a, tbk=tbk, n=n_in_batch:
                             e.tensor_copy(pT_sb[par][:, 512 * a:512 * a + 128 * n],
                                           bank_b16(tbk)[:, 0:128 * n]),
                             waits=[(PE, tv)], inc=VE)
                    bank_guard[("pt", tbk)] = [(VE, ev)]
                    ev_ve = ev
            pt_done = ([(AC, ev_ac)] if ev_ac else []) + ([(VE, ev_ve)] if ev_ve else [])
            abk = 6 + (idx % 2)
            mms = []
            for tb in range(ntb):
                def fn(e, tb=tb, par=par, hkv=hkv, abk=abk, ntb=ntb):
                    return e.matmul(bank(abk)[:, 0:128],
                                    v_tm[:, 512 * tb + 128 * hkv:512 * tb + 128 * (hkv + 1)],
                                    pT_sb[par][:, 128 * tb:128 * (tb + 1)],
                                    start=(tb == 0), stop=(tb == ntb - 1))
                mms.append(fn)
            attv = mm_group(abk, mms, extra_waits=pt_done)
            attval[idx] = attv
            ew = [bank_ready[abk]]
            if flags["first_att"]:
                ew.append((PE, a2a_last))
                flags["first_att"] = False
            aev = P.s(lambda e, head=head, qt=qt, abk=abk:
                      e.copy(attT[:, 1024 * head + 128 * qt:1024 * head + 128 * (qt + 1)],
                             bank(abk)[:, 0:128]),
                      waits=ew, inc=AC)
            bank_done(abk, AC, aev)
            return attv

        items = [(qt, head) for qt in range(8) for head in range(NH)]
        pend = None
        last_att_pe = 0
        for idx, (qt, head) in enumerate(items):
            stage1(idx, qt, head)
            if pend is not None:
                last_att_pe = stage2(pend)
            pend = idx
        last_att_pe = stage2(pend)
        return last_att_pe, AC.n

    def phase_a3(h, attT_AC, a2b_last):
        scrw = [(PE, a2b_last), (VE, VE.n)] + guards["scr"]
        guards["scr"] = []
        wo_val = [0] * NC_DM

        def load_wo(fo, waits=()):
            P.g(lambda e, fo=fo: e.dma_start(
                out=wo_sb[fo % 2], in_=wo[fo].rearrange("p c n -> p (c n)")),
                waits=waits, inc=wS[fo % 2], k=16)
            wo_val[fo] = wS[fo % 2].n

        load_wo(0, waits=guards["tab"])
        load_wo(1)
        guards["tab"] = []
        xr_rot = [0]
        xr_guard = [None, None]
        xr_v = [0, 0]
        first = True
        last_mm = 0
        last_add = 0
        for fo in range(NC_DM):
            for scol in range(2):
                jr = xr_rot[0] % 2
                xr_rot[0] += 1
                xw = [xr_guard[jr]] if xr_guard[jr] else []
                P.y(lambda e, fo=fo, scol=scol, jr=jr, h=h: e.dma_start(
                    out=xr_sb[jr],
                    in_=xT_r[fo][:, HALF * h + 512 * scol:HALF * h + 512 * (scol + 1)]),
                    waits=xw + (scrw if first else []), inc=xR[jr], k=16)
                xr_v[jr] = xR[jr].n
                xr_val = (xR[jr], xr_v[jr])
                bk = rot_sc[0] % 4
                rot_sc[0] += 1
                mms = []
                for c in range(NC_DM):
                    def fn(e, c=c, fo=fo, scol=scol, bk=bk):
                        return e.matmul(bank(bk), wo_sb[fo % 2][:, 128 * c:128 * (c + 1)],
                                        attT[:, 1024 * c + 512 * scol:1024 * c + 512 * (scol + 1)],
                                        start=(c == 0), stop=(c == NC_DM - 1))
                    mms.append(fn)
                ew = [(wS[fo % 2], wo_val[fo])] + ([(AC, attT_AC)] if scol == 0 else [])
                if first:
                    first = False
                last_mm = mm_group(bk, mms, extra_waits=ew)
                aw = [(PE, last_mm), xr_val]
                if fo == 0 and scol == 0:
                    aw += guards["acc"]
                    guards["acc"] = []
                last_add = P.v(lambda e, fo=fo, scol=scol, bk=bk, jr=jr: e.tensor_tensor(
                    acc[:, 1024 * fo + 512 * scol:1024 * fo + 512 * (scol + 1)],
                    bank(bk), xr_sb[jr], op=ALU.add),
                    waits=aw, inc=VE)
                bank_done(bk, VE, last_add)
                xr_guard[jr] = (VE, last_add)
            if fo + 2 < NC_DM:
                load_wo(fo + 2, waits=[(PE, last_mm)])
        if debug == "a3pre":
            dump(acc, 0, [(VE, last_add)])
            return None
        ydram_v = [0, 0]
        ln_last_tb = [0]
        prev_tb_pe = [a2b_last]
        ybf_first = [True]
        last_ybf_ve = [0]

        def layer_norm(h, write_back, a3_last_add):
            out_dma_val = [0, 0]
            for sch in range(8):
                tm = tmA if sch % 2 == 0 else tmB
                other = tmB if sch % 2 == 0 else tmA
                tin = 4 if sch % 2 == 0 else 6
                tout = 5 if sch % 2 == 0 else 7
                evv = 0
                for a in range(4):
                    for j in range(4):
                        f = 4 * a + j
                        waits = []
                        if j == 0:
                            waits = list(bank_guard.get(tin, []))
                            if a == 0 and sch == 0:
                                waits.append((VE, a3_last_add))
                            if a == 0 and sch >= 2:
                                waits.append((PE, ln_last_tb[0]))
                                waits.append((xR[sch % 2], ydram_v[sch % 2]))
                        tv = P.t(lambda e, f=f, sch=sch, tin=tin, j=j:
                                 e.transpose(bank(tin)[:, 128 * j:128 * (j + 1)],
                                             acc[:, 1024 * f + 128 * sch:1024 * f + 128 * (sch + 1)],
                                             idf),
                                 waits=waits, inc=PE if j == 3 else None)
                    evw = [(PE, tv)]
                    if sch == 0 and a == 0:
                        evw += scrw
                    evv = P.s(lambda e, a=a, tm=tm, tin=tin:
                              e.copy(tm[:, 512 * a:512 * (a + 1)], bank(tin)),
                              waits=evw, inc=AC)
                    bank_guard[tin] = [(AC, evv)]
                vs1 = P.v(lambda e, tm=tm: e.reduce_sum(st(16), tm, axis=AX.X),
                          waits=[(AC, evv)], inc=VE)
                sqw = [(PE, prev_tb_pe[0])] + (scrw if sch == 0 else [])
                if sch >= 1:
                    sqw.append((xR[(sch + 1) % 2], ydram_v[(sch + 1) % 2]))
                asq = P.s(lambda e, tm=tm, other=other: e.activation(
                    other, tm, AF.Square, accum_out=st(17)), waits=sqw, inc=AC)
                asq = P.s(lambda e: e.copy(st(25), st(25)), waits=[(AC, asq)], inc=AC)
                v1 = P.v(lambda e: e.tensor_scalar(st(18), st(16), 1.0 / DM, None, op0=ALU.mult),
                         waits=[(VE, vs1)], inc=VE)
                v2 = P.v(lambda e: e.tensor_scalar(st(19), st(17), 1.0 / DM, None, op0=ALU.mult),
                         waits=[(AC, asq), (VE, v1)], inc=VE)
                v3 = P.v(lambda e: e.tensor_tensor(st(20), st(18), st(18), op=ALU.mult),
                         waits=[(VE, v2)], inc=VE)
                vvar = P.v(lambda e: e.tensor_tensor(st(21), st(19), st(20), op=ALU.subtract),
                           waits=[(VE, v3)], inc=VE)
                P.s(lambda e: e.activation(st(22), st(21), AF.Ln, bias=st(24)),
                    waits=[(VE, vvar)])
                arstd = P.s(lambda e: e.activation(st(23), st(22), AF.Exp, scale=-0.5),
                            inc=AC)
                vnorm = P.v(lambda e, tm=tm: e.tensor_scalar(
                    tm, tm, st(18), st(23), op0=ALU.subtract, op1=ALU.mult),
                    waits=[(AC, arstd)], inc=VE)
                row0 = HALF * h + 128 * sch
                P.g(lambda e, row0=row0, tm=tm: e.dma_start(
                    out=y_dram[row0:row0 + 128, :], in_=tm),
                    waits=[(VE, vnorm)], inc=xR[sch % 2], k=16)
                P.y(lambda e, row0=row0, tm=tm: e.dma_start(
                    out=acc_dram[row0:row0 + 128, :], in_=tm),
                    waits=[(VE, vnorm)], inc=xR[sch % 2], k=16)
                ydram_v[sch % 2] = xR[sch % 2].n
                if debug == "ln1" and h == 0:
                    P.v(lambda e, sch=sch: e.tensor_copy(
                        wtok[:, 16 * sch:16 * sch + 9], stats[:, 16:25]),
                        waits=[(VE, vnorm)], inc=VE)
                if write_back:
                    for a in range(4):
                        for j in range(4):
                            f = 4 * a + j
                            waits = []
                            if j == 0:
                                waits = list(bank_guard.get(tout, []))
                                if a == 0:
                                    waits.append((VE, vnorm))
                            tv = P.t(lambda e, f=f, tm=tm, tout=tout, j=j:
                                     e.transpose(bank(tout)[:, 128 * j:128 * (j + 1)],
                                                 tm[:, 128 * f:128 * (f + 1)], idf),
                                     waits=waits, inc=PE if j == 3 else None)
                        ln_last_tb[0] = tv
                        accv = acc.rearrange("p (f c) -> p f c", f=NC_DM)
                        ybfv = ybf.rearrange("p (f c) -> p f c", f=NC_DM)
                        psv = bank(tout).rearrange("p (f c) -> p f c", f=4)
                        af = P.s(lambda e, a=a, sch=sch, accv=accv, psv=psv:
                                 e.copy(accv[:, 4 * a:4 * (a + 1), 128 * sch:128 * (sch + 1)], psv),
                                 waits=[(PE, tv)], inc=AC)
                        yw = [(AC, af)]
                        if ybf_first[0]:
                            yw += [(PE, a2b_last)]
                            ybf_first[0] = False
                        yv = P.v(lambda e, a=a, sch=sch, ybfv=ybfv, psv=psv:
                                 e.tensor_copy(ybfv[:, 4 * a:4 * (a + 1), 128 * sch:128 * (sch + 1)], psv),
                                 waits=yw, inc=VE)
                        last_ybf_ve[0] = yv
                        bank_guard[tout] = [(VE, yv)]
                    prev_tb_pe[0] = ln_last_tb[0]
            return last_ybf_ve[0]

        layer_norm(h, True, last_add)
        return last_mm, last_ybf_ve[0], ln_last_tb[0], layer_norm

    def phase_router(h, ybf_ready_ve):
        wtok_ve = []
        first = True
        for sch in range(8):
            mms = []
            for c in range(NC_DM):
                def fn(e, c=c, sch=sch):
                    return e.matmul(bank(0)[:, 0:32],
                                    ybf[:, 1024 * c + 128 * sch:1024 * c + 128 * (sch + 1)],
                                    rt_sb[:, 32 * c:32 * (c + 1)],
                                    start=(c == 0), stop=(c == NC_DM - 1))
                mms.append(fn)
            ew = [(VE, ybf_ready_ve), (PR, pr_val)] if first else []
            first = False
            mmv = mm_group(0, mms, extra_waits=ew)
            vm = P.v(lambda e: e.reduce_max(st(8), bank(0)[:, 0:32], axis=AX.X),
                     waits=[(PE, mmv)], inc=VE)
            vn = P.v(lambda e: e.tensor_scalar(st(9), st(8), -1.0, None, op0=ALU.mult),
                     waits=[(VE, vm)], inc=VE)
            ae = P.s(lambda e: e.activation(rp, bank(0)[:, 0:32], AF.Exp, bias=st(9)),
                     waits=[(VE, vn)], inc=AC)
            bank_done(0, AC, ae)
            bank_done(0, VE, vn)
            vq = P.v(lambda e: e.tensor_copy(rcur, rp), waits=[(AC, ae)], inc=VE)
            for r in range(4):
                vq = P.v(lambda e: e.reduce_max(st(10), rcur, axis=AX.X),
                         waits=[(VE, vq)], inc=VE)
                if r < 3:
                    vq = P.v(lambda e: e.tensor_scalar(rkeep, rcur, st(10), None, op0=ALU.is_ge),
                             waits=[(VE, vq)], inc=VE)
                    vq = P.v(lambda e: e.scalar_tensor_tensor(rcur, rkeep, -2.0, rcur,
                                                              op0=ALU.mult, op1=ALU.add),
                             waits=[(VE, vq)], inc=VE)
            vq = P.v(lambda e: e.tensor_scalar(rkeep, rp, st(10), None, op0=ALU.is_ge),
                     waits=[(VE, vq)], inc=VE)
            vq = P.v(lambda e: e.tensor_tensor(rp, rp, rkeep, op=ALU.mult),
                     waits=[(VE, vq)], inc=VE)
            vq = P.v(lambda e: e.reduce_sum(st(11), rp, axis=AX.X), waits=[(VE, vq)], inc=VE)
            vq = P.v(lambda e: e.reciprocal(st(12), st(11)), waits=[(VE, vq)], inc=VE)
            wv_ = P.v(lambda e, sch=sch: e.tensor_scalar(
                wtok[:, 32 * sch:32 * (sch + 1)], rp, st(12), None, op0=ALU.mult),
                waits=[(VE, vq)], inc=VE)
            wtok_ve.append(wv_)
            row0 = HALF * h + 128 * sch
            P.g(lambda e, sch=sch, row0=row0: e.dma_start(
                out=wtok_dram[row0:row0 + 128, :], in_=wtok[:, 32 * sch:32 * (sch + 1)]),
                waits=[(VE, wv_)], inc=sD, k=16)
        wtt_ac = 0
        for sch in range(8):
            waits = list(bank_guard.get(1, []))
            if sch == 0:
                waits.append((VE, wtok_ve[-1]))
            tv = P.t(lambda e, sch=sch: e.transpose(
                bank(1)[0:32, 0:128], wtok[:, 32 * sch:32 * (sch + 1)], idf),
                waits=waits, inc=PE)
            bank_ready[1] = (PE, tv)
            wtt_ac = P.s(lambda e, sch=sch: e.copy(
                wtokT[:, 128 * sch:128 * (sch + 1)], bank(1)[0:32, 0:128]),
                waits=[(PE, tv)], inc=AC)
            bank_guard[1] = [(AC, wtt_ac)]
        return wtt_ac

    sD_init = [0]

    def perm_init():
        itile = cv(P_ITL, 512, mybir.dt.int32)
        vi = P.v(lambda e: e.memset(itile, 10 ** 9), inc=VE)
        P.g(lambda e: e.dma_start(
            out=perm_dram[0:NE * CAP].rearrange("(p a) one -> p (a one)", p=128), in_=itile),
            waits=[(VE, vi)], inc=sD, k=16)
        sD_init[0] = sD.n

    def phase_idxbuild(h, wtt_ac):
        vk = P.v(lambda e: e.tensor_scalar(wtokT, wtokT, 0.0, None, op0=ALU.is_gt),
                 waits=[(AC, wtt_ac)], inc=VE)
        init = 0.0 if h == 0 else cntprev
        vscan = P.v(lambda e, init=init: e.tensor_tensor_scan(
            wtokT, wtokT, wtokT, init, op0=ALU.add, op1=ALU.bypass),
            waits=[(VE, vk)], inc=VE)
        vcnt = P.v(lambda e: e.tensor_copy(cntprev, wtokT[:, 1023:1024]),
                   waits=[(VE, vscan)], inc=VE)
        P.v(lambda e: e.tensor_scalar(st(15), iota_sb, float(NE * CAP), None, op0=ALU.add),
            waits=[(VE, vcnt)])
        scat_prev = [0, 0]
        pos_read = [0]
        for sc in range(8):
            scg = 8 * h + sc
            tbk = 6 + sc % 2
            tw = list(bank_guard.get(tbk, []))
            if sc == 0:
                tw.append((VE, vcnt))
            tv = P.t(lambda e, sc=sc, tbk=tbk: e.transpose(
                bank(tbk)[:, 0:32], wtokT[:, 128 * sc:128 * (sc + 1)], idf[0:32, 0:32]),
                waits=tw, inc=PE)
            bank_ready[tbk] = (PE, tv)
            pv = P.s(lambda e, tbk=tbk: e.copy(pos_tm, bank(tbk)[:, 0:32]),
                     waits=[(PE, tv), (VE, pos_read[0])], inc=AC)
            bank_guard[tbk] = [(AC, pv)]
            vv = P.v(lambda e, sc=sc: e.tensor_scalar(
                rcur, wtok[:, 32 * sc:32 * (sc + 1)], 0.0, None, op0=ALU.is_gt),
                waits=[(AC, pv)], inc=VE)
            vv = P.v(lambda e: e.tensor_scalar(ovm, pos_tm, CAP + 0.5, None, op0=ALU.is_le),
                     waits=[(VE, vv)], inc=VE)
            vv = P.v(lambda e: e.tensor_tensor(rcur, rcur, ovm, op=ALU.mult),
                     waits=[(VE, vv)], inc=VE)
            vv = P.v(lambda e: e.tensor_tensor(rp, pos_tm, ebase_sb, op=ALU.add),
                     waits=[(VE, vv)], inc=VE)
            pos_read[0] = vv
            vv = P.v(lambda e: e.tensor_scalar(rp, rp, -1.0, None, op0=ALU.add),
                     waits=[(VE, vv)], inc=VE)
            vv = P.v(lambda e: e.tensor_scalar(rkeep, rcur, -1.0e9, 1.0e9,
                                               op0=ALU.mult, op1=ALU.add),
                     waits=[(VE, vv)], inc=VE)
            vv = P.v(lambda e: e.tensor_tensor(rp, rp, rcur, op=ALU.mult),
                     waits=[(VE, vv)], inc=VE)
            vv = P.v(lambda e: e.tensor_tensor(rp, rp, rkeep, op=ALU.add),
                     waits=[(VE, vv)], inc=VE)
            vv = P.v(lambda e, scg=scg: e.tensor_scalar(
                st(9), iota_sb, float(128 * scg), None, op0=ALU.add),
                waits=[(VE, vv)], inc=VE)
            tkw = [(VE, vv)]
            if sc >= 2:
                tkw.append((sD, scat_prev[sc % 2]))
            vv = P.v(lambda e, sc=sc: e.tensor_copy(tok_i32[sc % 2], st(9)),
                     waits=tkw, inc=VE)
            for j in range(4):
                sl = 4 * (sc % 2) + j
                vv = P.v(lambda e: e.tensor_reduce(st(8), rp, axis=AX.X, op=ALU.min),
                         waits=[(VE, vv)], inc=VE)
                vv = P.v(lambda e: e.tensor_tensor(st(12), st(8), st(15), op=ALU.min),
                         waits=[(VE, vv)], inc=VE)
                vv = P.v(lambda e, sl=sl: e.tensor_copy(mj_i32[sl], st(12)),
                         waits=[(VE, vv)], inc=VE)
                vv = P.v(lambda e: e.tensor_scalar(rkeep, rp, st(8), None, op0=ALU.is_le),
                         waits=[(VE, vv)], inc=VE)
                vv = P.v(lambda e: e.scalar_tensor_tensor(rp, rkeep, 1.0e9, rp,
                                                          op0=ALU.mult, op1=ALU.add),
                         waits=[(VE, vv)], inc=VE)
                P.g(lambda e, sc=sc, sl=sl: e.indirect_dma_start(
                    perm_dram, IndirectOffsetOnAxis(mj_i32[sl], 0), tok_i32[sc % 2], None),
                    waits=[(VE, vv), (sD, sD_init[0])], inc=sD, k=16)
            scat_prev[sc % 2] = sD.n
        return sD.n

    def phase_sparse_b(bpool, ydram_rdy, wtok_rdy, perm_rdy, accdram_rdy, nexp=NE):
        NHF = NC_I // 2
        gA = [cv(12288 * sl_, 12288, BF16) for sl_ in range(3)]
        uA = [cv(36864 + 12288 * sl_, 12288, BF16) for sl_ in range(3)]
        dA = [cv(73728 + 12288 * ih, 12288, BF16) for ih in range(2)]
        y_gath = [cv(98304 + 16384 * par, 16384, BF16) for par in range(2)]
        yT_gath = cv(131072, 16384, BF16)
        hT2 = cv(147456, 6144, BF16)
        out_sb = [cv(153600 + 8192 * tp, 8192, F32) for tp in range(4)]
        wg_sb = [cv(186368 + 512 * par, 512, F32) for par in range(2)]
        idxi = [cv(187392 + 32 * par, 16, mybir.dt.int32) for par in range(2)]
        idxg = [cv(187456 + 32 * par, 16, mybir.dt.int32) for par in range(2)]
        idxa = [cv(187520 + 32 * par, 16, mybir.dt.int32) for par in range(2)]
        idxf = cv(187584, 16, F32)
        sg2 = cv(187648, 2048, F32)
        P.v(lambda e: e.tensor_scalar(st(11), iota_sb, float(S), None, op0=ALU.add))

        g_v = {}
        u_v = {}
        d_v = {}
        gu_guard = [[], [], []]
        dn_guard = [[], []]
        gath_guard = [[], []]
        idx_v = [0, 0]
        gath_v = [0, 0]
        scat_last = [0]

        def load_gu(e_, ih, waits=()):
            sl = slice(NHF * ih, NHF * (ih + 1))
            sb = (2 * e_ + ih) % 3
            P.g(lambda en, e_=e_, sl=sl, sb=sb: en.dma_start(
                out=gA[sb].rearrange("p (i cn) -> p i cn", i=NHF),
                in_=gate[e_, sl].rearrange("i p c n -> p i (c n)")),
                waits=list(waits) + gu_guard[sb], inc=wS[sb], k=16)
            g_v[(e_, ih)] = (wS[sb], wS[sb].n)
            P.g(lambda en, e_=e_, sl=sl, sb=sb: en.dma_start(
                out=uA[sb].rearrange("p (i cn) -> p i cn", i=NHF),
                in_=up[e_, sl].rearrange("i p c n -> p i (c n)")),
                inc=uS[sb], k=16)
            u_v[(e_, ih)] = (uS[sb], uS[sb].n)

        def load_dn(e_, ih, waits=()):
            sl = slice(NHF * ih, NHF * (ih + 1))
            P.g(lambda en, e_=e_, sl=sl, ih=ih: en.dma_start(
                out=dA[ih].rearrange("p (i f) -> p i f", i=NHF),
                in_=down[e_, sl].rearrange("i p f -> p i f")),
                waits=list(waits) + dn_guard[ih], inc=dS[ih], k=16)
            d_v[(e_, ih)] = dS[ih].n

        def load_gathers(e_, waits=()):
            par = e_ % 2
            for t in range(4):
                P.y(lambda en, e_=e_, t=t: en.dma_start(
                    out=idxi[par][:, t:t + 1],
                    in_=perm_dram[CAP * e_ + 128 * t:CAP * e_ + 128 * (t + 1)]),
                    waits=(list(waits) + gath_guard[par]) if t == 0 else [],
                    inc=xR[par], k=16)
            idx_v[par] = xR[par].n
            vv = P.v(lambda en, par=par: en.tensor_copy(idxf, idxi[par]),
                     waits=[(xR[par], idx_v[par])], inc=VE)
            vv = P.v(lambda en, par=par: en.tensor_scalar(idxa[par].bitcast(F32), idxf,
                                                          st(11), None, op0=ALU.min),
                     waits=[(VE, vv)], inc=VE)
            vv = P.v(lambda en, par=par: en.tensor_copy(idxa[par], idxa[par].bitcast(F32)),
                     waits=[(VE, vv)], inc=VE)
            vv = P.v(lambda en: en.tensor_scalar(idxf, idxf, float(S - 1), None,
                                                 op0=ALU.min), waits=[(VE, vv)], inc=VE)
            vv = P.v(lambda en, par=par: en.tensor_copy(idxg[par], idxf),
                     waits=[(VE, vv)], inc=VE)
            for t in range(4):
                P.g(lambda en, par=par, t=t: en.indirect_dma_start(
                    y_gath[par][:, 2048 * t:2048 * (t + 1)], None,
                    y_dram, IndirectOffsetOnAxis(idxg[par][:, t:t + 1], 0)),
                    waits=([(VE, vv)] + list(ydram_rdy) if t == 0 else []),
                    inc=OS[par], k=16)
            for t in range(4):
                P.g(lambda en, par=par, t=t: en.indirect_dma_start(
                    wg_sb[par][:, 32 * t:32 * (t + 1)], None,
                    wtok_dram, IndirectOffsetOnAxis(idxg[par][:, t:t + 1], 0)),
                    waits=(list(wtok_rdy) if t == 0 else []), inc=OS[par], k=16)
            gath_v[par] = OS[par].n

        first_w = [(sD, perm_rdy)]
        load_gathers(0, waits=first_w)
        load_gathers(1)
        load_gu(0, 0, waits=bpool)
        load_gu(0, 1)
        load_gu(1, 0)
        load_dn(0, 0)
        load_dn(0, 1)
        last_acc = 0
        for e_ in range(nexp):
            par = e_ % 2
            tpos = 0
            for t in range(4):
                for a in range(4):
                    tbk = (t * 4 + a) % 4
                    for j in range(4):
                        c = 4 * a + j
                        tw = []
                        if j == 0:
                            tw = list(bank_guard.get(tbk, []))
                            if t == 0 and a == 0:
                                tw += [(OS[par], gath_v[par])] + gu_guard[par]
                        tv = P.t(lambda en, par=par, t=t, c=c, tbk=tbk, j=j:
                                 en.transpose(bank_b16(tbk)[:, 128 * j:128 * (j + 1)],
                                              y_gath[par][:, 2048 * t + 128 * c:
                                                          2048 * t + 128 * (c + 1)], idb),
                                 waits=tw, inc=PE if j == 3 else None)
                    ytv = yT_gath.rearrange("p (c s) -> p c s", c=NC_DM)
                    psv = bank_b16(tbk)[:, 0:512].rearrange("p (c s) -> p c s", c=4)
                    ev = P.s(lambda en, par=par, t=t, a=a, ytv=ytv, psv=psv:
                             en.copy(ytv[:, 4 * a:4 * (a + 1), 128 * t:128 * (t + 1)], psv),
                             waits=[(PE, tv)], inc=AC)
                    bank_guard[tbk] = [(AC, ev)]
            tr_last = tv
            yT_rdy = AC.n
            last_h_ve = 0
            for ih in range(2):
                last_gu = 0
                for il in range(NHF):
                    i = NHF * ih + il
                    gbk = (i % 2) * 2
                    ubk = gbk + 1
                    mg = []
                    mu = []
                    sb = (2 * e_ + ih) % 3
                    for c in range(NC_DM):
                        def fg(en, c=c, il=il, sb=sb, par=par, gbk=gbk):
                            return en.matmul(bank(gbk),
                                             gA[sb][:, 2048 * il + 128 * c:2048 * il + 128 * (c + 1)],
                                             yT_gath[:, 512 * c:512 * (c + 1)],
                                             start=(c == 0), stop=(c == NC_DM - 1))
                        def fu(en, c=c, il=il, sb=sb, par=par, ubk=ubk):
                            return en.matmul(bank(ubk),
                                             uA[sb][:, 2048 * il + 128 * c:2048 * il + 128 * (c + 1)],
                                             yT_gath[:, 512 * c:512 * (c + 1)],
                                             start=(c == 0), stop=(c == NC_DM - 1))
                        mg.append(fg)
                        mu.append(fu)
                    ew = [g_v[(e_, ih)]]
                    if il == 0 and ih == 0:
                        ew.append((AC, yT_rdy))
                    gv = mm_group(gbk, mg, extra_waits=ew)
                    uv = mm_group(ubk, mu, extra_waits=[u_v[(e_, ih)]])
                    last_gu = uv
                    asil = P.s(lambda en, gbk=gbk: en.activation(sg2, bank(gbk), AF.Silu),
                               waits=[(PE, gv), (VE, last_h_ve)], inc=AC)
                    bank_done(gbk, AC, asil)
                    last_h_ve = P.v(lambda en, i=i, ubk=ubk: en.tensor_tensor(
                        hT2[:, 512 * i:512 * (i + 1)], sg2, bank(ubk), op=ALU.mult),
                        waits=[(PE, uv), (AC, asil)], inc=VE)
                    bank_done(ubk, VE, last_h_ve)
                gu_guard[(2 * e_ + ih) % 3] = [(PE, last_gu)]
                nh = 2 * e_ + ih + 3
                if nh // 2 < nexp:
                    load_gu(nh // 2, nh % 2, waits=[(PE, last_gu)])
            last_dn = 0
            wr_prev = scat_last[0]
            for t in range(4):
                ev_vals = []
                for fc in range(4):
                    bk = 4 + fc
                    mms = []
                    for ic in range(NC_I):
                        def fn(en, ic=ic, t=t, fc=fc, par=par, bk=bk):
                            return en.matmul(bank(bk),
                                             hT2[:, 512 * ic + 128 * t:512 * ic + 128 * (t + 1)],
                                             dA[ic // NHF][:, 2048 * (ic % NHF) + 512 * fc:
                                                           2048 * (ic % NHF) + 512 * (fc + 1)],
                                             start=(ic == 0), stop=(ic == NC_I - 1))
                        mms.append(fn)
                    ew = [(dS[0], d_v[(e_, 0)]), (dS[1], d_v[(e_, 1)])]
                    if t == 0 and fc == 0:
                        ew.append((VE, last_h_ve))
                    dv = mm_group(bk, mms, extra_waits=ew)
                    last_dn = dv
                    vw = [(PE, dv)] + ([(WR, wr_prev)] if fc == 0 else [])
                    vv = P.s(lambda en, t=t, fc=fc, par=par, bk=bk, e_=e_:
                             en.activation(out_sb[t][:, 512 * fc:512 * (fc + 1)],
                                           bank(bk), AF.Copy,
                                           scale=wg_sb[par][:, 32 * t + e_:32 * t + e_ + 1]),
                             waits=vw, inc=AC)
                    bank_done(bk, AC, vv)
                    ev_vals.append(vv)
                sw = [(AC, ev_vals[-1]), (WR, wr_prev)]
                if e_ == 0 and t == 0:
                    sw += list(accdram_rdy)
                P.g(lambda en, t=t, par=par: en.indirect_dma_start(
                    acc_dram, IndirectOffsetOnAxis(idxa[par][:, t:t + 1], 0),
                    out_sb[t], None, compute_op=ALU.add),
                    waits=sw, inc=WR, k=16)
            scat_last[0] = WR.n
            dn_guard[0] = [(PE, last_dn)]
            dn_guard[1] = [(PE, last_dn)]
            gath_guard[par] = [(PE, tr_last), (WR, scat_last[0])]
            if e_ + 2 < nexp:
                load_gathers(e_ + 2)
            if e_ + 1 < nexp:
                load_dn(e_ + 1, 0)
                load_dn(e_ + 1, 1)
        return scat_last[0], PE.n

    def phase_ln2_dram(scat_done, b_pe):
        tm2 = [cv(16384 * p_, 8192, F32) for p_ in range(2)]
        sq_scr = cv(24576, 8192, F32)
        out_v = [0, 0]
        in_v = [0, 0]
        for sch in range(16):
            par = sch % 2
            tm = tm2[par]
            iw = [(WR, scat_done), (PE, b_pe)] if sch < 2 else                  [(OS[par], out_v[par])]
            P.y(lambda e, sch=sch, tm=tm: e.dma_start(
                out=tm, in_=acc_dram[128 * sch:128 * (sch + 1), :]),
                waits=iw, inc=xR[par], k=16)
            in_v[par] = xR[par].n
            vs1 = P.v(lambda e, tm=tm: e.reduce_sum(st(16), tm, axis=AX.X),
                      waits=[(xR[par], in_v[par])], inc=VE)
            sqw = [] if sch >= 1 else [(PE, b_pe)]
            sqw += [(xR[par], in_v[par])]
            asq = P.s(lambda e, tm=tm: e.activation(
                sq_scr, tm, AF.Square, accum_out=st(17)), waits=sqw, inc=AC)
            asq = P.s(lambda e: e.copy(st(25), st(25)), waits=[(AC, asq)], inc=AC)
            v1 = P.v(lambda e: e.tensor_scalar(st(18), st(16), 1.0 / DM, None, op0=ALU.mult),
                     waits=[(VE, vs1)], inc=VE)
            v2 = P.v(lambda e: e.tensor_scalar(st(19), st(17), 1.0 / DM, None, op0=ALU.mult),
                     waits=[(AC, asq), (VE, v1)], inc=VE)
            v3 = P.v(lambda e: e.tensor_tensor(st(20), st(18), st(18), op=ALU.mult),
                     waits=[(VE, v2)], inc=VE)
            vvar = P.v(lambda e: e.tensor_tensor(st(21), st(19), st(20), op=ALU.subtract),
                       waits=[(VE, v3)], inc=VE)
            P.s(lambda e: e.activation(st(22), st(21), AF.Ln, bias=st(24)), waits=[(VE, vvar)])
            arstd = P.s(lambda e: e.activation(st(23), st(22), AF.Exp, scale=-0.5), inc=AC)
            vnorm = P.v(lambda e, tm=tm: e.tensor_scalar(
                tm, tm, st(18), st(23), op0=ALU.subtract, op1=ALU.mult),
                waits=[(AC, arstd)], inc=VE)
            P.y(lambda e, sch=sch, tm=tm: e.dma_start(out=outr[sch], in_=tm),
                waits=[(VE, vnorm)], inc=OS[par], k=16)
            out_v[par] = OS[par].n

    perm_init()
    a2a_last = phase_a2a(0)
    if debug == "qTr":
        b0 = dump(qTr, 0, [(VE, VE.n)])
        return finish_debug()
    if debug == "att00":
        phase_a2b(0, a2a_last)
        return finish_debug()
    a2b_last, attT_AC = phase_a2b(0, a2a_last)
    guards["tab"] = [(PE, a2b_last), (VE, VE.n)]
    if debug == "attT":
        b0 = dump(attT, 0, [(AC, attT_AC)])
        return finish_debug()
    if debug == "a3pre":
        phase_a3(0, attT_AC, a2b_last)
        return finish_debug()
    a3_last_mm, ybf_ve, ln_tb, layer_norm_fn = phase_a3(0, attT_AC, a2b_last)
    if debug == "ln1":
        b0 = dump(acc, 0, [(PE, ln_tb), (AC, AC.n), (VE, VE.n)])
        dump(ybf, b0, [(VE, ybf_ve)])
        return finish_debug()
    wtt_ac = phase_router(0, ybf_ve)
    if debug == "wtok":
        b0 = dump(wtok, 0, [(VE, VE.n)])
        P.g(lambda e: e.dma_start(out=outr[b0][0:32, 0:1024], in_=wtokT),
            waits=[(AC, wtt_ac)], inc=DG, k=16)
        return finish_debug()
    phase_idxbuild(0, wtt_ac)

    guards["xatt"] = [(PE, PE.n)]
    guards["ybf"] = [(PE, PE.n)]
    guards["acc"] = [(PE, PE.n)]
    guards["scr"] = [(PE, PE.n), (VE, VE.n)]
    P.y(lambda e: e.dma_start(out=cosf, in_=cosd), waits=[(PE, a2b_last)], inc=TB, k=16)
    P.y(lambda e: e.dma_start(out=sinf, in_=sind), inc=TB, k=16)
    guards["tab"] = guards["tab"] + [(TB, TB.n)]
    tabrdy[0] = [(TB, TB.n)]
    guards["kv"] = []

    a2a_last = phase_a2a(1)
    a2b_last, attT_AC = phase_a2b(1, a2a_last)
    guards["tab"] = [(PE, a2b_last), (VE, VE.n)]
    a3_last_mm, ybf_ve, ln_tb, _ = phase_a3(1, attT_AC, a2b_last)
    wtt_ac = phase_router(1, ybf_ve)
    perm_rdy = phase_idxbuild(1, wtt_ac)
    if debug == "perm":
        P.g(lambda e: e.dma_start(
            out=out[0:8, :], in_=perm_dram[0:NE * CAP].rearrange("(r c) one -> r (c one)", r=8)),
            waits=[(sD, perm_rdy)], inc=DG, k=16)
        P.g(lambda e: e.dma_start(out=out[8:40, 0:1024], in_=wtokT),
            waits=[(VE, VE.n)], inc=DG, k=16)
        return finish_debug()

    bpool = [(PE, PE.n), (AC, AC.n), (VE, VE.n),
             (xR[0], xR[0].n), (xR[1], xR[1].n)]
    ydram_rdy = [(xR[0], xR[0].n), (xR[1], xR[1].n)]
    wtok_rdy = [(sD, sD.n)]
    accdram_rdy = [(xR[0], xR[0].n), (xR[1], xR[1].n)]
    nexp = 1 if debug == "acc1" else NE
    scat_done, b_pe = phase_sparse_b(bpool, ydram_rdy, wtok_rdy, perm_rdy,
                                     accdram_rdy, nexp=nexp)
    if debug == "acc1":
        for a_ in range(16):
            P.g(lambda e, a_=a_: e.dma_start(
                out=outr[a_], in_=acc_dram[128 * a_:128 * (a_ + 1), :]),
                waits=[(WR, scat_done)] if a_ == 0 else [], inc=DG, k=16)
        return finish_debug()
    phase_ln2_dram(scat_done, b_pe)

    P.y(lambda e: e.nop(), waits=[(OS[0], OS[0].n), (OS[1], OS[1].n)])
    P.finalize()
    return nc



def _host_prep(L0_attn_x, L0_attn_Wqkv, L0_attn_Wo, L0_ffn_router_w,
               L0_ffn_gate_w, L0_ffn_up_w, L0_ffn_down_w):
    f32 = np.float32
    Wqkv = np.asarray(L0_attn_Wqkv, f32)
    Wo = np.asarray(L0_attn_Wo, f32)
    router = np.asarray(L0_ffn_router_w, f32)
    gate_w = np.asarray(L0_ffn_gate_w, f32)
    up_w = np.asarray(L0_ffn_up_w, f32)
    down_w = np.asarray(L0_ffn_down_w, f32)

    com = {}
    wq_m = Wqkv[:, :NH * HD]
    com["wq"] = np.ascontiguousarray(
        wq_m.reshape(NC_DM, 128, NH, 128).transpose(2, 1, 0, 3))
    wk_m = Wqkv[:, NH * HD:(NH + NKV) * HD]
    com["wk"] = np.ascontiguousarray(
        wk_m.reshape(NC_DM, 128, NKV, 128).transpose(2, 1, 0, 3))
    wv_m = Wqkv[:, (NH + NKV) * HD:]
    com["wv"] = np.ascontiguousarray(
        wv_m.reshape(NC_DM, 128, NKV * 128).transpose(1, 0, 2))
    com["wo"] = np.ascontiguousarray(
        Wo.reshape(NC_DM, 128, NC_DM, 128).transpose(2, 1, 0, 3))
    com["rt"] = np.ascontiguousarray(
        router.T.reshape(NC_DM, 128, NE).transpose(1, 0, 2))
    com["gate"] = np.ascontiguousarray(
        gate_w.reshape(NE, NC_I, 128, NC_DM, 128).transpose(0, 1, 4, 3, 2))
    com["up"] = np.ascontiguousarray(
        up_w.reshape(NE, NC_I, 128, NC_DM, 128).transpose(0, 1, 4, 3, 2))
    com["down"] = np.ascontiguousarray(
        down_w.reshape(NE, DM, NC_I, 128).transpose(0, 2, 3, 1))

    j = np.arange(64, dtype=np.float64)
    freqs = 1.0 / THETA ** (2.0 * j / HD)
    ang = np.arange(S, dtype=np.float64)[None, :] * freqs[:, None]
    cos64 = np.cos(ang).astype(f32)
    sin64 = np.sin(ang).astype(f32)
    com["cosd"] = np.concatenate([cos64, cos64], axis=0)
    com["sind"] = np.concatenate([-sin64, sin64], axis=0)
    qi = np.arange(128)[:, None]
    tj = np.arange(128)[None, :]
    com["trid"] = np.where(tj > qi, np.float32(MASKNEG), np.float32(0.0)).astype(f32)
    com["identd"] = np.eye(128, dtype=f32)
    com["iotad"] = np.arange(128, dtype=f32)[:, None]
    com["ebased"] = np.broadcast_to(
        (np.arange(NE, dtype=f32) * CAP)[None, :], (128, NE)).copy()

    x = np.asarray(L0_attn_x, f32)
    xTs = [np.ascontiguousarray(x[b].T) for b in range(B)]
    return com, xTs


def kernel(L0_attn_x, L0_attn_Wqkv, L0_attn_Wo, L0_ffn_router_w,
           L0_ffn_gate_w, L0_ffn_up_w, L0_ffn_down_w):
    com, xTs = _host_prep(L0_attn_x, L0_attn_Wqkv, L0_attn_Wo, L0_ffn_router_w,
                          L0_ffn_gate_w, L0_ffn_up_w, L0_ffn_down_w)
    nc = build_program()
    in_maps = [dict(com, xT=xTs[b]) for b in range(B)]
    res = run_bass_kernel_spmd(nc, in_maps, core_ids=list(range(B)))
    return np.stack([r["out"] for r in res.results], axis=0)
